# version 03 (final)
# baseline (speedup 1.0000x reference)
import numpy as np
import concourse.bass as bass
import concourse.mybir as mybir
from concourse.bass_utils import run_bass_kernel_spmd
from contextlib import ExitStack

F32 = mybir.dt.float32
BF16 = mybir.dt.bfloat16
AF = mybir.ActivationFunctionType
ALU = mybir.AluOpType
AX = mybir.AxisListType

NCORES = 8
BPC = 2
SEQ = 2048
T = BPC * SEQ
D = 1024

ENG_STREAMS = ("pe", "act", "dve")
DMA_STREAMS = ("sp", "po")
ALL_STREAMS = ENG_STREAMS + DMA_STREAMS
KSUB = 8




PSUM_MAP = {
    ((128, 2048), F32, 0): ("A", 0, 0),
    ((128, 1024), F32, 0): ("B", 0, 0),
    ((128, 1024), F32, 1): ("A", 0, 0),
    ((64, 1024), F32, 0): ("A", 0, 0),
    ((128, 512), F32, 0): ("C", 0, 0),
    ((128, 512), F32, 1): ("A", 0, 0),
    ((128, 512), F32, 2): ("A", 0, 512),
    ((128, 192), F32, 0): ("A", 0, 1024),
    ((64, 128), F32, 0): ("A", 0, 1216),
    ((128, 128), F32, 0): ("A", 0, 1792),
    ((128, 8), F32, 0): ("A", 0, 1536),
    ((128, 128), BF16, 0): ("Dbf", 0, 0),
    ((128, 128), BF16, 1): ("Dbf", 0, 128),
}


class PsView:
    def __init__(self, base, r0, c0, shape):
        self._base = base
        self._r0 = r0
        self._c0 = c0
        self.shape = shape

    def __getitem__(self, idx):
        if idx == slice(None):
            rs, cs = slice(0, self.shape[0]), slice(0, self.shape[1])
        else:
            rs, cs = idx
            if rs == slice(None):
                rs = slice(0, self.shape[0])
            if cs == slice(None):
                cs = slice(0, self.shape[1])
        return self._base[self._r0 + rs.start:self._r0 + rs.stop,
                          self._c0 + cs.start:self._c0 + cs.stop]


class P:

    def __init__(self, name="prog"):
        self.nc = bass.Bass("TRN2", target_bir_lowering=False)
        self.stack = ExitStack()
        self.ops = {s: [] for s in ALL_STREAMS}
        self.cnt = {s: 0 for s in ALL_STREAMS}
        self.state = {}
        self.waited = {s: {t: 0 for t in ALL_STREAMS} for s in ALL_STREAMS}
        self.sems = {}
        self.uid = 0
        self.cpool_dram = self.nc.dram_tensor("cpool", [128, 256], F32,
                                              kind="ExternalInput")
        self.cpool = self.stack.enter_context(
            self.nc.sbuf_tensor("cpool_sb", [128, 256], F32))
        self.cvals = {}
        self.pooling = False
        self.sbuf_pool = {}
        self.psum_arena = None
        self.stage_ord = {}
        self.dma(self.cpool[:], self.cpool_dram.ap()[:, :], writes=[self.cpool])

    def enable_pooling(self):
        self.pooling = True
        self.psum_arena = {
            "A": self.stack.enter_context(
                self.nc.psum_tensor("ps_arena_A", [128, 2048], F32)),
            "B": self.stack.enter_context(
                self.nc.psum_tensor("ps_arena_B", [128, 1024], F32)),
            "C": self.stack.enter_context(
                self.nc.psum_tensor("ps_arena_C", [128, 512], F32)),
            "Dbf": self.stack.enter_context(
                self.nc.psum_tensor("ps_arena_D", [128, 256], BF16)),
        }

    def begin_stage(self):
        self.stage_ord = {}

    def const(self, val, dtype=F32):
        val = float(val)
        key = (dtype, val)
        if key not in self.cvals:
            i = len(self.cvals)
            assert i < 256
            self.cvals[key] = i
            self.nc.const_aps.register(dtype, val, self.cpool[:, i:i + 1])
        return val

    def cpool_host(self):
        arr = np.zeros((128, 256), dtype=np.float32)
        for (dt_, v), i in self.cvals.items():
            arr[:, i] = v
        return arr

    def sbuf(self, name, shape, dtype):
        if self.pooling:
            k0 = ("sb", tuple(shape), dtype)
            ordn = self.stage_ord.get(k0, 0)
            self.stage_ord[k0] = ordn + 1
            key = (tuple(shape), dtype, ordn)
            if key not in self.sbuf_pool:
                self.sbuf_pool[key] = self.stack.enter_context(
                    self.nc.sbuf_tensor(f"pool_{len(self.sbuf_pool)}",
                                        list(shape), dtype))
            return self.sbuf_pool[key]
        h = self.stack.enter_context(self.nc.sbuf_tensor(name, list(shape), dtype))
        return h

    def psum(self, name, shape, dtype=F32):
        if self.pooling:
            k0 = ("ps", tuple(shape), dtype)
            ordn = self.stage_ord.get(k0, 0)
            self.stage_ord[k0] = ordn + 1
            key = (tuple(shape), dtype, ordn)
            if key not in PSUM_MAP:
                raise RuntimeError(f"no psum map for {key}")
            an, r0, c0 = PSUM_MAP[key]
            return PsView(self.psum_arena[an], r0, c0, tuple(shape))
        h = self.stack.enter_context(self.nc.psum_tensor(name, list(shape), dtype))
        return h

    def dram(self, name, shape, dtype, kind="Internal"):
        return self.nc.dram_tensor(name, list(shape), dtype, kind=kind)

    def _st(self, buf):
        if isinstance(buf, PsView):
            buf = buf._base
        k = id(buf)
        if k not in self.state:
            self.state[k] = {"w": None, "r": []}
        return self.state[k]

    def _wait(self, s, dep):
        t, c = dep
        if t in DMA_STREAMS:
            sub = (c - 1) % KSUB
            need = ((c - 1) // KSUB + 1) * 16
            key = (t, sub)
        else:
            need = c
            key = t
        w = self.waited[s]
        if w.get(key, 0) < need:
            self.ops[s].append(("wait", key, need))
            w[key] = need

    def emit(self, s, fn, reads=(), writes=()):
        for b in reads:
            st = self._st(b)
            if st["w"] is not None:
                self._wait(s, st["w"])
        for b in writes:
            st = self._st(b)
            if st["w"] is not None:
                self._wait(s, st["w"])
            for r in st["r"]:
                self._wait(s, r)
        self.cnt[s] += 1
        c = self.cnt[s]
        if s in DMA_STREAMS:
            sub = (c - 1) % KSUB
            ordn = (c - 1) // KSUB + 1
            if ordn > 1:
                key = (s, sub)
                need = (ordn - 1) * 16
                if self.waited[s].get(key, 0) < need:
                    self.ops[s].append(("wait", key, need))
                    self.waited[s][key] = need
            self.ops[s].append(("op", fn, sub))
        else:
            self.ops[s].append(("op", fn))
        for b in reads:
            self._st(b)["r"].append((s, c))
        for b in writes:
            self.state[id(b)] = {"w": (s, c), "r": []}
        return (s, c)

    def barrier_outputs(self, s="act"):
        for t in DMA_STREAMS:
            for sub in range(KSUB):
                n = (self.cnt[t] - sub + KSUB - 1) // KSUB if self.cnt[t] > sub else 0
                n = len([c for c in range(1, self.cnt[t] + 1) if (c - 1) % KSUB == sub])
                if n > 0:
                    key = (t, sub)
                    need = n * 16
                    if self.waited[s].get(key, 0) < need:
                        self.ops[s].append(("wait", key, need))
                        self.waited[s][key] = need

    def build(self):
        nc = self.nc
        for s in ENG_STREAMS:
            self.sems[s] = self.stack.enter_context(nc.semaphore(f"sem_{s}"))
        for s in DMA_STREAMS:
            for sub in range(KSUB):
                self.sems[(s, sub)] = self.stack.enter_context(
                    nc.semaphore(f"sem_{s}{sub}"))
        block = self.stack.enter_context(nc.Block())
        me = self

        def mk_replay(s):
            def _replay(eng):
                for item in me.ops[s]:
                    if item[0] == "wait":
                        _, key, need = item
                        eng.wait_ge(me.sems[key], need)
                    elif len(item) == 3:
                        inst = item[1](eng)
                        inst.then_inc(me.sems[(s, item[2])], 16)
                    else:
                        inst = item[1](eng)
                        inst.then_inc(me.sems[s], 1)
            return _replay

        block.sync(mk_replay("sp"))
        block.gpsimd(mk_replay("po"))
        block.tensor(mk_replay("pe"))
        block.scalar(mk_replay("act"))
        block.vector(mk_replay("dve"))
        self.stack.close()
        return nc

    def dma(self, dst, src, reads=(), writes=(), q="sp"):
        reads = list(reads)
        writes = list(writes)
        try:
            if dst.tensor is not None:
                writes.append(dst.tensor)
            if src.tensor is not None:
                reads.append(src.tensor)
        except AttributeError:
            pass
        return self.emit(q, lambda e: e.dma_start(dst, src), reads=reads, writes=writes)

    def act(self, out, in_, func, bias=None, scale=None, accum_out=None,
            reads=(), writes=()):
        kw = {}
        reads = list(reads)
        if bias is not None:
            if isinstance(bias, (int, float)):
                self.const(bias)
                reads.append(self.cpool)
            kw["bias"] = bias
        if scale is not None:
            if isinstance(scale, (int, float)):
                self.const(scale)
                reads.append(self.cpool)
            kw["scale"] = scale
        if accum_out is not None:
            kw["accum_out"] = accum_out
        return self.emit("act", lambda e: e.activation(out, in_, func, **kw),
                         reads=reads, writes=writes)

    def tt(self, out, a, b, op, reads=(), writes=()):
        return self.emit("dve", lambda e: e.tensor_tensor(out, a, b, op),
                         reads=reads, writes=writes)

    def ts(self, out, in_, s1, op0, s2=None, op1=None, reads=(), writes=()):
        if s2 is None:
            return self.emit("dve", lambda e: e.tensor_scalar(out, in_, s1, None, op0),
                             reads=reads, writes=writes)
        return self.emit("dve", lambda e: e.tensor_scalar(out, in_, s1, s2, op0, op1),
                         reads=reads, writes=writes)

    def stt(self, out, in0, scalar, in1, op0, op1, reads=(), writes=()):
        return self.emit("dve",
                         lambda e: e.scalar_tensor_tensor(out, in0, scalar, in1, op0, op1),
                         reads=reads, writes=writes)

    def red(self, out, in_, op, negate=None, reads=(), writes=()):
        kw = {}
        if negate is not None:
            kw["negate"] = negate
        return self.emit("dve",
                         lambda e: e.tensor_reduce(out, in_, AX.X, op, **kw),
                         reads=reads, writes=writes)

    def copy(self, out, in_, reads=(), writes=(), eng="dve"):
        return self.emit(eng, lambda e: e.tensor_copy(out, in_),
                         reads=reads, writes=writes)

    def scopy(self, out, in_, reads=(), writes=()):
        return self.emit("act", lambda e: e.activation(out, in_, AF.Copy),
                         reads=reads, writes=writes)

    def memset(self, ap, val, reads=(), writes=()):
        return self.emit("dve", lambda e: e.memset(ap, val), reads=reads, writes=writes)

    def mm(self, out, lhsT, rhs, start=True, stop=True, reads=(), writes=()):
        return self.emit("pe", lambda e: e.matmul(out, lhsT, rhs, start=start, stop=stop),
                         reads=reads, writes=writes)

    def trans(self, out, in_, ident, reads=(), writes=()):
        return self.emit("pe", lambda e: e.transpose(out, in_, ident),
                         reads=reads, writes=writes)

    def scan(self, out, data0, data1, initial, reads=(), writes=()):
        reads = list(reads)
        if isinstance(initial, (int, float)):
            self.const(float(initial))
            reads.append(self.cpool)
        return self.emit("dve",
                         lambda e: e.tensor_tensor_scan(out, data0, data1, initial,
                                                        ALU.mult, ALU.add),
                         reads=reads, writes=writes)



def common_consts(p, dr):
    c = {}
    c["id32"] = p.sbuf("c_id32", (128, 128), F32)
    c["idbf"] = p.sbuf("c_idbf", (128, 128), BF16)
    c["ones_bf"] = p.sbuf("c_ones_bf", (1, 128), BF16)
    c["ones_f"] = p.sbuf("c_ones_f", (1, 128), F32)
    c["ones_col"] = p.sbuf("c_ones_col", (128, 1), F32)
    p.dma(c["id32"][:], dr["ident32"].ap()[:, :], writes=[c["id32"]])
    p.dma(c["idbf"][:], dr["identbf"].ap()[:, :], writes=[c["idbf"]])
    p.dma(c["ones_bf"][:], dr["ones_bf"].ap()[:, :], writes=[c["ones_bf"]])
    p.dma(c["ones_f"][:], dr["ones_f"].ap()[:, :], writes=[c["ones_f"]])
    p.dma(c["ones_col"][:], dr["ones_col"].ap()[:, :], writes=[c["ones_col"]])
    return c


import os


def emit_glue(p, c, x_ap, s_ap, g_ap, out_x, out_xn, out_xnT32, out_xnTbf, tag="", ntile=None):
    nc = p.nc
    xt = p.sbuf(f"gl_x{tag}", (128, D), F32)
    st_ = p.sbuf(f"gl_s{tag}", (128, D), F32)
    gt = p.sbuf(f"gl_g{tag}", (128, D), F32)
    xn = p.sbuf(f"gl_xn{tag}", (128, D), F32)
    xnb = p.sbuf(f"gl_xnb{tag}", (128, D), BF16)
    sq = p.sbuf(f"gl_sq{tag}", (128, D), F32)
    col = p.sbuf(f"gl_col{tag}", (128, 8), F32)
    ps = p.psum(f"gl_ps{tag}", (128, 128), F32)
    tb = p.sbuf(f"gl_tb{tag}", (128, 128), F32)
    tbb = p.sbuf(f"gl_tbb{tag}", (128, 128), BF16)
    if ntile is None:
        ntile = T // 128
    for it in range(ntile):
        r = slice(it * 128, (it + 1) * 128)
        p.dma(xt[:], x_ap[r, :], writes=[xt])
        if s_ap is not None:
            p.dma(st_[:], s_ap[r, :], writes=[st_])
            p.dma(gt[:], g_ap[r, :], writes=[gt])
            p.tt(st_[:], st_[:], gt[:], ALU.mult, reads=[st_, gt], writes=[st_])
            p.tt(xt[:], xt[:], st_[:], ALU.add, reads=[xt, st_], writes=[xt])
        if out_x is not None:
            p.dma(out_x[r, :], xt[:], reads=[xt], q="po")
        su = col[:, 0:1]
        ss = col[:, 1:2]
        mu = col[:, 2:3]
        var = col[:, 3:4]
        rstd = col[:, 5:6]
        p.red(su, xt[:], ALU.add, reads=[xt], writes=[col])
        p.act(sq[:], xt[:], AF.Square, accum_out=ss, reads=[xt, col], writes=[sq, col])
        p.ts(mu, su, 1.0 / D, ALU.mult, reads=[col], writes=[col])
        p.tt(var, mu, mu, ALU.mult, reads=[col], writes=[col])
        p.stt(var, ss, 1.0 / D, var, ALU.mult, ALU.subtract, reads=[col], writes=[col])
        p.act(col[:, 4:5], var, AF.Ln, bias=1e-5, reads=[col], writes=[col])
        p.act(rstd, col[:, 4:5], AF.Exp, scale=-0.5, reads=[col], writes=[col])
        p.ts(xn[:], xt[:], mu, ALU.subtract, rstd, ALU.mult,
             reads=[xt, col], writes=[xn])
        if out_xn is not None:
            p.dma(out_xn[r, :], xn[:], reads=[xn], q="po")
        if out_xnTbf is None or os.environ.get("GLUE_NOTRANS"):
            continue
        for j in range(8):
            cs = slice(j * 128, (j + 1) * 128)
            p.trans(ps[:], xn[:, cs], c["id32"][:], reads=[xn, c["id32"]], writes=[ps])
            p.copy(tb[:], ps[:], reads=[ps], writes=[tb])
            if out_xnT32 is not None:
                p.dma(out_xnT32[cs, r], tb[:], reads=[tb], q="po")
            if os.environ.get("GLUE_NOBF"):
                continue
            p.copy(tbb[:], ps[:], reads=[ps], writes=[tbb])
            p.dma(out_xnTbf[cs, r], tbb[:], reads=[tbb], q="po")



def wload(p, name, dram, R, C, out_dt=BF16, stage=None):
    tiles = []
    own_stage = stage is None
    if own_stage and out_dt != F32:
        stage = p.sbuf(f"{name}_stg", (128, C), F32)
    ntl = (R + 127) // 128
    for i in range(ntl):
        rows = min(128, R - i * 128)
        tb = p.sbuf(f"{name}_{i}", (rows, C), out_dt)
        if out_dt == F32:
            p.dma(tb[:], dram.ap()[i * 128:i * 128 + rows, :], writes=[tb])
        else:
            p.dma(stage[0:rows, :], dram.ap()[i * 128:i * 128 + rows, :],
                  writes=[stage])
            p.copy(tb[:], stage[0:rows, :], reads=[stage], writes=[tb])
        tiles.append(tb)
    return tiles


def bias_cols(p, name, dram, N):
    t = p.sbuf(f"{name}_c", (128, max(1, N // 128)), F32)
    for j in range(N // 128):
        p.dma(t[:, j:j + 1], dram.ap()[j * 128:(j + 1) * 128].unsqueeze(-1),
              writes=[t])
    return t


def brow(p, name, dram, N, out_dt=F32):
    t32 = p.sbuf(f"{name}_r32", (1, N), F32)
    p.dma(t32[:], dram.ap()[:].unsqueeze(0), writes=[t32])
    if out_dt == F32:
        return t32
    tb = p.sbuf(f"{name}_r", (1, N), out_dt)
    p.copy(tb[:], t32[:], reads=[t32], writes=[tb])
    return tb


def rank1(p, ps, ones_f, row_f32, start=False, stop=False, reads=(), writes=()):
    p.mm(ps, ones_f[0:1, :], row_f32, start=start, stop=stop,
         reads=list(reads) + [ones_f], writes=list(writes))


def mm_acc(p, ps_ap, pairs, ps_key, extra_first=None):
    n = len(pairs)
    for i, (l, r, rk) in enumerate(pairs):
        p.mm(ps_ap, l, r, start=(i == 0), stop=(i == n - 1),
             reads=rk, writes=[ps_key])


def ln_tok(p, out_ap, in_ap, col, sq, eps, wcol=None, bcol=None,
           rkeys=(), wkeys=(), n=D):
    su, ss, mu, var, lv, rstd = (col[:, 0:1], col[:, 1:2], col[:, 2:3],
                                 col[:, 3:4], col[:, 4:5], col[:, 5:6])
    p.red(su, in_ap, ALU.add, reads=list(rkeys), writes=[col])
    p.act(sq[:], in_ap, AF.Square, accum_out=ss, reads=list(rkeys) + [col],
          writes=[sq, col])
    p.ts(mu, su, 1.0 / n, ALU.mult, reads=[col], writes=[col])
    p.tt(var, mu, mu, ALU.mult, reads=[col], writes=[col])
    p.stt(var, ss, 1.0 / n, var, ALU.mult, ALU.subtract, reads=[col], writes=[col])
    p.act(lv, var, AF.Ln, bias=float(eps), reads=[col], writes=[col])
    p.act(rstd, lv, AF.Exp, scale=-0.5, reads=[col], writes=[col])
    p.ts(out_ap, in_ap, mu, ALU.subtract, rstd, ALU.mult,
         reads=list(rkeys) + [col], writes=list(wkeys))


def softplus(p, out_ap, in_ap, tmp1, tmp2, rkeys, wkeys, tmpkeys):
    p.act(tmp1, in_ap, AF.Abs, reads=list(rkeys), writes=[tmpkeys[0]])
    p.act(tmp1, tmp1, AF.Exp, scale=-1.0, reads=[tmpkeys[0]], writes=[tmpkeys[0]])
    p.act(tmp1, tmp1, AF.Ln, bias=1.0, reads=[tmpkeys[0]], writes=[tmpkeys[0]])
    p.ts(tmp2, in_ap, 0.0, ALU.max, reads=list(rkeys), writes=[tmpkeys[1]])
    p.tt(out_ap, tmp1, tmp2, ALU.add, reads=tmpkeys, writes=list(wkeys))


def transpose_out(p, c, src_ap, dst_dram_ap, ps, stg, rkeys, ncol=8,
                  dt=F32, stg2=None, dst2=None):
    ident = c["id32"] if dt == F32 else c["idbf"]
    for j in range(ncol):
        p.trans(ps[:], src_ap[:, j * 128:(j + 1) * 128], ident[:],
                reads=list(rkeys) + [ident], writes=[ps])
        p.copy(stg[:], ps[:], reads=[ps], writes=[stg])
        p.dma(dst_dram_ap(j), stg[:], reads=[stg], q="po")
        if stg2 is not None:
            p.copy(stg2[:], ps[:], reads=[ps], writes=[stg2])
            p.dma(dst2(j), stg2[:], reads=[stg2], q="po")



def mmS(p, ps_t, prows, pc0, lhsT_ap, rhs_t, rrows, rc0, w, start, stop,
        reads, writes, wchunk=512):
    for off in range(0, w, wchunk):
        ww = min(wchunk, w - off)
        p.mm(ps_t[prows, pc0 + off:pc0 + off + ww], lhsT_ap,
             rhs_t[rrows, rc0 + off:rc0 + off + ww],
             start=start, stop=stop, reads=reads, writes=writes)


def rank1S(p, ps_t, prows, pc0, ones_f, row_t, rc0, w, start, stop, reads, writes):
    for off in range(0, w, 512):
        ww = min(512, w - off)
        p.mm(ps_t[prows, pc0 + off:pc0 + off + ww], ones_f[0:1, :],
             row_t[0:1, rc0 + off:rc0 + off + ww],
             start=start, stop=stop, reads=list(reads) + [ones_f], writes=writes)



def emit_m1(p, c, io, li):
    resid = io["resid_in"]
    rmsw = brow(p, "rmsw", io[f"w_l0_rmsw{li}"], D)
    rmsb = p.sbuf("rmsb", (128, D), F32)
    ps_b = p.psum("ps_b", (128, D), F32)
    rank1S(p, ps_b, slice(None), 0, c["ones_f"], rmsw, 0, D, True, True,
           [rmsw], [ps_b])
    p.copy(rmsb[:], ps_b[:], reads=[ps_b], writes=[rmsb])
    inpT = wload(p, "inpT", io[f"w_l0_inpT{li}"], D, 4096)
    xt = p.sbuf("m1_x", (128, D), F32)
    xn = p.sbuf("m1_xn", (128, D), F32)
    xnb = p.sbuf("m1_xnb", (128, D), BF16)
    col = p.sbuf("m1_col", (128, 8), F32)
    ps_t = p.psum("m1_pst", (128, 128), BF16)
    stg = p.sbuf("m1_stg", (128, 128), BF16)
    nT_sb = [p.sbuf(f"m1_nT{i}", (128, T), BF16) for i in range(8)]
    ntile = T // 128
    for it in range(ntile):
        r = slice(it * 128, (it + 1) * 128)
        p.dma(xt[:], resid.ap()[r, :], writes=[xt])
        ss, lv, rstd = col[:, 1:2], col[:, 4:5], col[:, 5:6]
        p.act(xn[:], xt[:], AF.Square, accum_out=ss, reads=[xt], writes=[xn, col])
        p.act(lv, ss, AF.Ln, scale=1.0 / D, bias=1e-5, reads=[col], writes=[col])
        p.act(rstd, lv, AF.Exp, scale=-0.5, reads=[col], writes=[col])
        p.ts(xn[:], xt[:], rstd, ALU.mult, reads=[xt, col], writes=[xn])
        p.tt(xnb[:], xn[:], rmsb[:], ALU.mult, reads=[xn, rmsb], writes=[xnb])
        for j in range(8):
            p.trans(ps_t[:], xnb[:, j * 128:(j + 1) * 128], c["idbf"][:],
                    reads=[xnb, c["idbf"]], writes=[ps_t])
            p.copy(nT_sb[j][:, r], ps_t[:], reads=[ps_t], writes=[nT_sb[j]])
    TW = min(1024, T)
    ps = p.psum("m1_ps", (128, TW), F32)
    ev = p.sbuf("m1_ev", (128, TW), BF16)
    for et in range(32):
        es = slice(et * 128, (et + 1) * 128)
        for tc in range(T // TW):
            ts_ = slice(tc * TW, (tc + 1) * TW)
            for ci in range(8):
                mmS(p, ps, slice(None), 0, inpT[ci][:, es], nT_sb[ci],
                    slice(None), tc * TW, TW, ci == 0, ci == 7,
                    [inpT[ci], nT_sb[ci]], [ps])
            p.copy(ev[:], ps[:], reads=[ps], writes=[ev])
            p.dma(io["xzT"].ap()[es, ts_], ev[:], reads=[ev], q="po")


def emit_m2(p, c, io, li, dbg=None):
    cw = p.sbuf("cw", (128, 4 * 16), F32)
    for et in range(16):
        p.dma(cw[:, et * 4:(et + 1) * 4],
              io[f"w_l0_convw{li}"].ap()[et * 128:(et + 1) * 128, :], writes=[cw])
    cb = bias_cols(p, "cb", io[f"w_l0_convb{li}"], 2048)
    xin = p.sbuf("m2_xin", (128, SEQ + 3), BF16)
    acc = p.sbuf("m2_acc", (128, SEQ), F32)
    xc_sb = [p.sbuf(f"m2_xc{i}", (128, T), BF16) for i in range(16)]
    for et in range(16):
        es = slice(et * 128, (et + 1) * 128)
        for b in range(BPC):
            tsl = slice(b * SEQ, (b + 1) * SEQ)
            p.memset(xin[:, 0:3], 0.0, writes=[xin])
            p.dma(xin[:, 3:], io["xzT"].ap()[es, tsl], writes=[xin])
            p.ts(acc[:], xin[:, 0:SEQ], cw[:, et * 4:et * 4 + 1], ALU.mult,
                 reads=[xin, cw], writes=[acc])
            for k in range(1, 4):
                p.stt(acc[:], xin[:, k:k + SEQ], cw[:, et * 4 + k:et * 4 + k + 1],
                      acc[:], ALU.mult, ALU.add, reads=[xin, cw, acc], writes=[acc])
            p.act(xc_sb[et][:, tsl], acc[:], AF.Silu, bias=cb[:, et:et + 1],
                  reads=[acc, cb], writes=[xc_sb[et]])
        p.dma(io["xcT"].ap()[es, :], xc_sb[et][:], reads=[xc_sb[et]], q="po")
    xpT = wload(p, "xpT", io[f"w_l0_xprojT{li}"], 2048, 96)
    ps_d = p.psum("m2_psd", (128, 512), F32)
    xdbl = p.sbuf("m2_xdbl", (96, T), F32)
    for tc in range(T // 512):
        ts_ = slice(tc * 512, (tc + 1) * 512)
        for ci in range(16):
            p.mm(ps_d[0:96, :], xpT[ci][:, :], xc_sb[ci][:, ts_],
                 start=(ci == 0), stop=(ci == 15),
                 reads=[xpT[ci], xc_sb[ci]], writes=[ps_d])
        p.copy(xdbl[:, ts_], ps_d[0:96, :], reads=[ps_d], writes=[xdbl])
    p.dma(io["bcBC"].ap()[:, :], xdbl[64:96, :], reads=[xdbl], q="po")
    dtw = wload(p, "dtw", io[f"w_l0_dtprojT{li}"], 64, 2048)
    dtb = bias_cols(p, "dtb", io[f"w_l0_dtb{li}"], 2048)
    dtr = p.sbuf("m2_dtr", (64, T), BF16)
    p.copy(dtr[:], xdbl[0:64, :], reads=[xdbl], writes=[dtr])
    TW = min(1024, T)
    ps2 = p.psum("m2_ps2", (128, TW), F32)
    pre = p.sbuf("m2_pre", (128, TW), F32)
    t1 = p.sbuf("m2_t1", (128, TW), F32)
    t2 = p.sbuf("m2_t2", (128, TW), F32)
    dd = p.sbuf("m2_dd", (128, TW), F32)
    for et in range(16):
        es = slice(et * 128, (et + 1) * 128)
        for tc in range(T // TW):
            ts_ = slice(tc * TW, (tc + 1) * TW)
            mmS(p, ps2, slice(None), 0, dtw[0][0:64, es], dtr, slice(None),
                tc * TW, TW, True, True, [dtw[0], dtr], [ps2])
            p.ts(pre[:], ps2[:], dtb[:, et:et + 1], ALU.add,
                 reads=[ps2, dtb], writes=[pre])
            if dbg is not None and et == 0 and tc == 0:
                p.dma(dbg[0].ap()[:, :], pre[:, 0:512], reads=[pre], q="po")
                dtr32 = p.sbuf("dtr32dbg", (64, 512), F32)
                p.copy(dtr32[:], dtr[:, 0:512], reads=[dtr], writes=[dtr32])
                p.dma(dbg[1].ap()[:, :], dtr32[:], reads=[dtr32], q="po")
            softplus(p, dd[:], pre[:], t1[:], t2[:], [pre], [dd], [t1, t2])
            p.dma(io["dT"].ap()[es, ts_], dd[:], reads=[dd], q="po")


def emit_m3(p, c, io, li):
    TC = min(1024, SEQ)
    alog = p.sbuf("m3_alog", (128, 16 * 16), F32)
    for et in range(16):
        p.dma(alog[:, et * 16:(et + 1) * 16],
              io[f"w_l0_alog{li}"].ap()[et * 128:(et + 1) * 128, :], writes=[alog])
    acol = p.sbuf("m3_acol", (128, 16 * 16), F32)
    p.act(acol[:], alog[:], AF.Exp, reads=[alog], writes=[acol])
    p.ts(acol[:], acol[:], -1.0, ALU.mult, reads=[acol], writes=[acol])
    dcol = bias_cols(p, "ddc", io[f"w_l0_dd{li}"], 2048)
    br32 = p.sbuf("m3_br32", (1, TC), F32)
    brb = p.sbuf("m3_brb", (1, TC), BF16)
    ps_bc = p.psum("m3_psbc", (128, TC), F32)
    Bb = [p.sbuf(f"m3_Bb{n}", (128, TC), BF16) for n in range(16)]
    Cb = [p.sbuf(f"m3_Cb{n}", (128, TC), BF16) for n in range(16)]
    xct = p.sbuf("m3_xct", (128, TC), BF16)
    zt = p.sbuf("m3_zt", (128, TC), BF16)
    dt_ = p.sbuf("m3_dt", (128, TC), F32)
    du = p.sbuf("m3_du", (128, TC), BF16)
    at = p.sbuf("m3_at", (128, TC), F32)
    bt = p.sbuf("m3_bt", (128, TC), F32)
    ht = p.sbuf("m3_ht", (128, TC), F32)
    ya = p.sbuf("m3_ya", (128, TC), F32)
    tmp = p.sbuf("m3_tmp", (128, TC), F32)
    sz = p.sbuf("m3_sz", (128, TC), F32)
    yb = p.sbuf("m3_yb", (128, TC), BF16)
    st = [p.sbuf(f"m3_st{i}", (128, 16), F32) for i in range(16)]
    for b in range(BPC):
        for tc in range(SEQ // TC):
            ts_ = slice(b * SEQ + tc * TC, b * SEQ + (tc + 1) * TC)
            t0_ = b * SEQ + tc * TC
            for n in range(16):
                p.dma(br32[:], io["bcBC"].ap()[n:n + 1, t0_:t0_ + TC],
                      writes=[br32])
                p.copy(brb[:], br32[:], reads=[br32], writes=[brb])
                mmS(p, ps_bc, slice(None), 0, c["ones_bf"][0:1, :], brb,
                    slice(0, 1), 0, TC, True, True,
                    [brb, c["ones_bf"]], [ps_bc])
                p.copy(Bb[n][:], ps_bc[:], reads=[ps_bc], writes=[Bb[n]])
                p.dma(br32[:], io["bcBC"].ap()[16 + n:17 + n, t0_:t0_ + TC],
                      writes=[br32])
                p.copy(brb[:], br32[:], reads=[br32], writes=[brb])
                mmS(p, ps_bc, slice(None), 0, c["ones_bf"][0:1, :], brb,
                    slice(0, 1), 0, TC, True, True,
                    [brb, c["ones_bf"]], [ps_bc])
                p.copy(Cb[n][:], ps_bc[:], reads=[ps_bc], writes=[Cb[n]])
            for et in range(16):
                es = slice(et * 128, (et + 1) * 128)
                p.dma(xct[:], io["xcT"].ap()[es, ts_], writes=[xct])
                p.dma(dt_[:], io["dT"].ap()[es, ts_], writes=[dt_])
                p.tt(du[:], dt_[:], xct[:], ALU.mult, reads=[dt_, xct], writes=[du])
                p.memset(ya[:], 0.0, writes=[ya])
                for n in range(16):
                    p.act(at[:], dt_[:], AF.Exp,
                          scale=acol[:, et * 16 + n:et * 16 + n + 1],
                          reads=[dt_, acol], writes=[at])
                    p.tt(bt[:], du[:], Bb[n][:], ALU.mult,
                         reads=[du, Bb[n]], writes=[bt])
                    if tc == 0:
                        p.scan(ht[:], at[:], bt[:], 0.0,
                               reads=[at, bt], writes=[ht])
                    else:
                        p.scan(ht[:], at[:], bt[:], st[et][:, n:n + 1],
                               reads=[at, bt, st[et]], writes=[ht])
                    if tc == 0:
                        p.copy(st[et][:, n:n + 1], ht[:, TC - 1:TC],
                               reads=[ht], writes=[st[et]])
                    p.tt(tmp[:], ht[:], Cb[n][:], ALU.mult,
                         reads=[ht, Cb[n]], writes=[tmp])
                    p.tt(ya[:], ya[:], tmp[:], ALU.add,
                         reads=[ya, tmp], writes=[ya])
                p.stt(ya[:], xct[:], dcol[:, et:et + 1], ya[:], ALU.mult,
                      ALU.add, reads=[xct, dcol, ya], writes=[ya])
                p.dma(zt[:], io["xzT"].ap()[2048 + et * 128: 2048 + (et + 1) * 128,
                                            ts_], writes=[zt])
                p.act(sz[:], zt[:], AF.Silu, reads=[zt], writes=[sz])
                p.tt(yb[:], ya[:], sz[:], ALU.mult, reads=[ya, sz], writes=[yb])
                p.dma(io["yT"].ap()[es, ts_], yb[:], reads=[yb], q="po")


def emit_m4(p, c, io, li, resid_add):
    opT = wload(p, "opT", io[f"w_l0_outprojT{li}"], 2048, 1024)
    lnw = brow(p, "lnw", io[f"w_l0_lnw{li}"], D)
    lnb = brow(p, "lnb", io[f"w_l0_lnb{li}"], D)
    wb = p.sbuf("m4_wb", (128, D), F32)
    bb = p.sbuf("m4_bb", (128, D), F32)
    ps_b = p.psum("m4_psb", (128, D), F32)
    rank1S(p, ps_b, slice(None), 0, c["ones_f"], lnw, 0, D, True, True,
           [lnw], [ps_b])
    p.copy(wb[:], ps_b[:], reads=[ps_b], writes=[wb])
    rank1S(p, ps_b, slice(None), 0, c["ones_f"], lnb, 0, D, True, True,
           [lnb], [ps_b])
    p.copy(bb[:], ps_b[:], reads=[ps_b], writes=[bb])
    yT = [p.sbuf(f"m4_yT{i}", (128, T), BF16) for i in range(16)]
    for i in range(16):
        p.dma(yT[i][:], io["yT"].ap()[i * 128:(i + 1) * 128, :], writes=[yT[i]])
    ps = p.psum("m4_ps", (128, 1024), F32)
    hv = p.sbuf("m4_hv", (128, 1024), F32)
    hn = p.sbuf("m4_hn", (128, 1024), F32)
    col = p.sbuf("m4_col", (128, 8), F32)
    sq = p.sbuf("m4_sq", (128, 1024), F32)
    rt = p.sbuf("m4_rt", (128, 1024), F32)
    for it in range(T // 128):
        r = slice(it * 128, (it + 1) * 128)
        for ci in range(16):
            mmS(p, ps, slice(None), 0, yT[ci][:, r], opT[ci], slice(None),
                0, 1024, ci == 0, ci == 15, [yT[ci], opT[ci]], [ps])
        p.copy(hv[:], ps[:], reads=[ps], writes=[hv])
        ln_tok(p, hn[:], hv[:], col, sq, 1e-5, rkeys=[hv], wkeys=[hn])
        p.tt(hn[:], hn[:], wb[:], ALU.mult, reads=[hn, wb], writes=[hn])
        p.tt(hn[:], hn[:], bb[:], ALU.add, reads=[hn, bb], writes=[hn])
        if resid_add:
            p.dma(rt[:], io["resid_in"].ap()[r, :], writes=[rt])
            p.tt(hn[:], hn[:], rt[:], ALU.add, reads=[hn, rt], writes=[hn])
        p.dma(io["resid_out"].ap()[r, :], hn[:], reads=[hn], q="po")


def emit_mffn_a(p, c, io):
    w1T = wload(p, "w1T", io["w_l0_ffw1T"], D, 8192)
    b1c = bias_cols(p, "b1c", io["w_l0_ffb1"], 8192)
    hf = p.sbuf("mf_hf", (128, 1024), F32)
    hfb = p.sbuf("mf_hfb", (128, 1024), BF16)
    ps_t = p.psum("mf_pst", (128, 128), BF16)
    TW = min(1024, T)
    hT = [p.sbuf(f"mf_hT{i}", (128, TW), BF16) for i in range(8)]
    ps_u = p.psum("mf_psu", (128, TW), F32)
    ps_g = p.psum("mf_psg", (128, TW), F32)
    av = p.sbuf("mf_av", (128, TW), F32)
    gv = p.sbuf("mf_gv", (128, TW), F32)
    vb = p.sbuf("mf_vb", (128, TW), BF16)
    for tc in range(T // TW):
        for st_ in range(TW // 128):
            r = slice(tc * TW + st_ * 128, tc * TW + (st_ + 1) * 128)
            p.dma(hf[:], io["hfinal"].ap()[r, :], writes=[hf])
            p.copy(hfb[:], hf[:], reads=[hf], writes=[hfb])
            for j in range(8):
                p.trans(ps_t[:], hfb[:, j * 128:(j + 1) * 128], c["idbf"][:],
                        reads=[hfb, c["idbf"]], writes=[ps_t])
                p.copy(hT[j][:, st_ * 128:(st_ + 1) * 128], ps_t[:],
                       reads=[ps_t], writes=[hT[j]])
        for fp in range(32):
            for ci in range(8):
                mmS(p, ps_u, slice(None), 0, w1T[ci][:, fp * 128:(fp + 1) * 128],
                    hT[ci], slice(None), 0, TW, ci == 0, ci == 7,
                    [w1T[ci], hT[ci]], [ps_u])
            for ci in range(8):
                mmS(p, ps_g, slice(None), 0,
                    w1T[ci][:, 4096 + fp * 128:4096 + (fp + 1) * 128],
                    hT[ci], slice(None), 0, TW, ci == 0, ci == 7,
                    [w1T[ci], hT[ci]], [ps_g])
            p.ts(av[:], ps_u[:], b1c[:, fp:fp + 1], ALU.add,
                 reads=[ps_u, b1c], writes=[av])
            p.act(gv[:], ps_g[:], AF.Gelu, bias=b1c[:, 32 + fp:33 + fp],
                  reads=[ps_g, b1c], writes=[gv])
            p.tt(vb[:], av[:], gv[:], ALU.mult, reads=[av, gv], writes=[vb])
            p.dma(io["vT"].ap()[fp * 128:(fp + 1) * 128,
                                tc * TW:(tc + 1) * TW], vb[:],
                  reads=[vb], q="po")


def emit_mffn_b(p, c, io):
    w2T = wload(p, "w2T", io["w_l0_ffw2T"], 4096, 1024)
    b2r = brow(p, "b2r", io["w_l0_ffb2"], D)
    TW = min(1024, T)
    vv = [p.sbuf(f"mb_vv{i}", (128, TW), BF16) for i in range(32)]
    ps_o = p.psum("mb_pso", (128, TW), F32)
    ov = p.sbuf("mb_ov", (128, TW), F32)
    ps_b = p.psum("mb_psb", (128, D), F32)
    b2b = p.sbuf("mb_b2b", (128, D), F32)
    rank1S(p, ps_b, slice(None), 0, c["ones_f"], b2r, 0, D, True, True,
           [b2r], [ps_b])
    p.copy(b2b[:], ps_b[:], reads=[ps_b], writes=[b2b])
    sT = p.sbuf("mb_sT", (128, 128), F32)
    ps_t2 = p.psum("mb_pst2", (128, 128), F32)
    so = p.sbuf("mb_so", (128, 1024), F32)
    for tc in range(T // TW):
        for fp in range(32):
            p.dma(vv[fp][:], io["vT"].ap()[fp * 128:(fp + 1) * 128,
                                           tc * TW:(tc + 1) * TW],
                  writes=[vv[fp]])
        for dt_ in range(8):
            ds = slice(dt_ * 128, (dt_ + 1) * 128)
            for fp in range(32):
                mmS(p, ps_o, slice(None), 0, w2T[fp][:, ds], vv[fp],
                    slice(None), 0, TW, fp == 0, fp == 31,
                    [w2T[fp], vv[fp]], [ps_o])
            p.copy(ov[:], ps_o[:], reads=[ps_o], writes=[ov])
            p.dma(io["s0T"].ap()[ds, tc * TW:(tc + 1) * TW], ov[:],
                  reads=[ov], q="po")
    for it in range(T // 128):
        r = slice(it * 128, (it + 1) * 128)
        for j in range(8):
            p.dma(sT[:], io["s0T"].ap()[j * 128:(j + 1) * 128, r], writes=[sT])
            p.trans(ps_t2[:], sT[:], c["id32"][:], reads=[sT, c["id32"]],
                    writes=[ps_t2])
            p.copy(so[:, j * 128:(j + 1) * 128], ps_t2[:], reads=[ps_t2],
                   writes=[so])
        p.tt(so[:], so[:], b2b[:], ALU.add, reads=[so, b2b], writes=[so])
        p.dma(io["s0"].ap()[r, :], so[:], reads=[so], q="po")



def _quant(p, c, name, dramT, R, C, gamma_n):
    stage = p.sbuf(name + "_stg", (128, C), F32)
    acc = p.sbuf(name + "_acc", (128, 1), F32)
    tmpc = p.sbuf(name + "_tc", (128, 1), F32)
    ntl = (R + 127) // 128
    p.memset(acc[:], 0.0, writes=[acc])
    for i in range(ntl):
        rows = min(128, R - i * 128)
        p.dma(stage[0:rows, :], dramT.ap()[i * 128:i * 128 + rows, :],
              writes=[stage])
        p.emit("dve", lambda e, rows=rows: e.tensor_reduce(
            tmpc[0:rows, 0:1], stage[0:rows, :], AX.X, ALU.add,
            apply_absolute_value=True), reads=[stage], writes=[tmpc])
        p.tt(acc[0:rows, :], acc[0:rows, :], tmpc[0:rows, :], ALU.add,
             reads=[acc, tmpc], writes=[acc])
    psg = p.psum(name + "_psg", (128, 8), F32)
    p.mm(psg[0:1, 0:1], acc[:, 0:1], c["ones_col"][:, 0:1], start=True,
         stop=True, reads=[acc, c["ones_col"]], writes=[psg])
    gam = p.sbuf(name + "_gam", (1, 2), F32)
    p.copy(gam[0:1, 0:1], psg[0:1, 0:1], reads=[psg], writes=[gam])
    p.act(gam[0:1, 1:2], gam[0:1, 0:1], AF.Identity, scale=0.5 / gamma_n,
          bias=0.5e-6, reads=[gam], writes=[gam])
    thrb = p.sbuf(name + "_thr", (128, 2), F32)
    rank1S(p, psg, slice(None), 0, c["ones_f"], gam, 0, 2, True, True,
           [gam], [psg])
    p.copy(thrb[:], psg[:, 0:2], reads=[psg], writes=[thrb])
    thrn = p.sbuf(name + "_thrn", (128, 1), F32)
    p.ts(thrn[:], thrb[:, 1:2], -1.0, ALU.mult, reads=[thrb], writes=[thrn])
    qt = []
    mp = p.sbuf(name + "_mp", (128, C), F32)
    for i in range(ntl):
        rows = min(128, R - i * 128)
        tb = p.sbuf(f"{name}_q{i}", (rows, C), BF16)
        p.dma(stage[0:rows, :], dramT.ap()[i * 128:i * 128 + rows, :],
              writes=[stage])
        p.ts(mp[0:rows, :], stage[0:rows, :], thrb[:, 1:2], ALU.is_gt,
             reads=[stage, thrb], writes=[mp])
        p.ts(stage[0:rows, :], stage[0:rows, :], thrn[:, 0:1], ALU.is_lt,
             reads=[stage, thrn], writes=[stage])
        p.tt(tb[:], mp[0:rows, :], stage[0:rows, :], ALU.subtract,
             reads=[mp, stage], writes=[tb])
        qt.append(tb)
    return qt


def emit_b0a(p, c, io):
    wq = _quant(p, c, "wfc", io["w_b0_WfcT"], D, 4096, 4096.0 * 1024.0)
    lnw = brow(p, "b0lnw", io["w_b0_lnw"], 4096)
    lnb = brow(p, "b0lnb", io["w_b0_lnb"], 4096)
    ps_w = p.psum("b0_psw", (128, 512), F32)
    wbt = [p.sbuf(f"b0_wb{i}", (128, 512), F32) for i in range(8)]
    bbt = [p.sbuf(f"b0_bb{i}", (128, 512), F32) for i in range(8)]
    for j in range(8):
        rank1S(p, ps_w, slice(None), 0, c["ones_f"], lnw, j * 512, 512,
               True, True, [lnw], [ps_w])
        p.copy(wbt[j][:], ps_w[:], reads=[ps_w], writes=[wbt[j]])
        rank1S(p, ps_w, slice(None), 0, c["ones_f"], lnb, j * 512, 512,
               True, True, [lnb], [ps_w])
        p.copy(bbt[j][:], ps_w[:], reads=[ps_w], writes=[bbt[j]])
    xnt = [p.sbuf(f"b0_x{i}", (128, 128), BF16) for i in range(8)]
    ps = p.psum("b0_ps", (128, 512), F32)
    h = p.sbuf("b0_h", (128, 4096), F32)
    col = p.sbuf("b0_col", (128, 8), F32)
    sq = p.sbuf("b0_sq", (128, 4096), F32)
    hb = p.sbuf("b0_hb", (128, 4096), BF16)
    ps_t = p.psum("b0_pst", (128, 128), BF16)
    stg = p.sbuf("b0_stg", (128, 128), BF16)
    for it in range(T // 128):
        r = slice(it * 128, (it + 1) * 128)
        for ci in range(8):
            p.dma(xnt[ci][:], io["xn0Tb"].ap()[ci * 128:(ci + 1) * 128, r],
                  writes=[xnt[ci]])
        for fc in range(8):
            fs = slice(fc * 512, (fc + 1) * 512)
            for ci in range(8):
                p.mm(ps[:], xnt[ci][:], wq[ci][:, fs], start=(ci == 0),
                     stop=(ci == 7), reads=[xnt[ci], wq[ci]], writes=[ps])
            p.act(h[:, fs], ps[:], AF.Gelu, reads=[ps], writes=[h])
        ln_tok(p, h[:], h[:], col, sq, 1e-5, rkeys=[h], wkeys=[h], n=4096)
        for j in range(8):
            fs = slice(j * 512, (j + 1) * 512)
            p.tt(h[:, fs], h[:, fs], wbt[j][:], ALU.mult, reads=[h, wbt[j]],
                 writes=[h])
            p.tt(h[:, fs], h[:, fs], bbt[j][:], ALU.add, reads=[h, bbt[j]],
                 writes=[h])
        p.copy(hb[:], h[:], reads=[h], writes=[hb])
        for j in range(32):
            p.trans(ps_t[:], hb[:, j * 128:(j + 1) * 128], c["idbf"][:],
                    reads=[hb, c["idbf"]], writes=[ps_t])
            p.copy(stg[:], ps_t[:], reads=[ps_t], writes=[stg])
            p.dma(io["bhT"].ap()[j * 128:(j + 1) * 128, r], stg[:],
                  reads=[stg], q="po")


def emit_b0b(p, c, io):
    wq = _quant(p, c, "wpr", io["w_b0_WprT"], 4096, 1024, 4096.0 * 1024.0)
    ps = p.psum("b1_ps", (128, 512), F32)
    ht = [p.sbuf(f"b1_h{i}", (128, 128), BF16) for i in range(32)]
    ov = p.sbuf("b1_ov", (128, 1024), F32)
    xt = p.sbuf("b1_xt", (128, 1024), F32)
    st_ = p.sbuf("b1_st", (128, 1024), F32)
    xn = p.sbuf("b1_xn", (128, 1024), F32)
    xnb = p.sbuf("b1_xnb", (128, 1024), BF16)
    col = p.sbuf("b1_col", (128, 8), F32)
    sq = p.sbuf("b1_sq", (128, 1024), F32)
    ps_t = p.psum("b1_pst", (128, 128), F32)
    tb = p.sbuf("b1_tb", (128, 128), F32)
    tbb = p.sbuf("b1_tbb", (128, 128), BF16)
    for it in range(T // 128):
        r = slice(it * 128, (it + 1) * 128)
        for ci in range(32):
            p.dma(ht[ci][:], io["bhT"].ap()[ci * 128:(ci + 1) * 128, r],
                  writes=[ht[ci]])
        for dc in range(2):
            ds = slice(dc * 512, (dc + 1) * 512)
            for ci in range(32):
                p.mm(ps[:], ht[ci][:], wq[ci][:, ds], start=(ci == 0),
                     stop=(ci == 31), reads=[ht[ci], wq[ci]], writes=[ps])
            p.act(ov[:, ds], ps[:], AF.Silu, reads=[ps], writes=[ov])
        p.dma(xt[:], io["x0"].ap()[r, :], writes=[xt])
        p.dma(st_[:], io["s0"].ap()[r, :], writes=[st_])
        p.tt(ov[:], ov[:], st_[:], ALU.mult, reads=[ov, st_], writes=[ov])
        p.tt(xt[:], xt[:], ov[:], ALU.add, reads=[xt, ov], writes=[xt])
        p.dma(io["x1"].ap()[r, :], xt[:], reads=[xt], q="po")
        ln_tok(p, xn[:], xt[:], col, sq, 1e-5, rkeys=[xt], wkeys=[xn])
        p.dma(io["xn1"].ap()[r, :], xn[:], reads=[xn], q="po")
        p.copy(xnb[:], xn[:], reads=[xn], writes=[xnb])
        for j in range(8):
            cs = slice(j * 128, (j + 1) * 128)
            p.trans(ps_t[:], xn[:, cs], c["id32"][:], reads=[xn, c["id32"]],
                    writes=[ps_t])
            p.copy(tb[:], ps_t[:], reads=[ps_t], writes=[tb])
            p.copy(tbb[:], ps_t[:], reads=[ps_t], writes=[tbb])
            p.dma(io["xn1T32"].ap()[cs, r], tb[:], reads=[tb], q="po")
            p.dma(io["xn1Tb"].ap()[cs, r], tbb[:], reads=[tbb], q="po")



def glue_tail(p, c, io, r, gv, bufs, layer):
    xt, st_, xn, col, sq, ps_t, tb, tbb = bufs
    p.dma(xt[:], io["x_in"].ap()[r, :], writes=[xt])
    p.dma(st_[:], io["s_in"].ap()[r, :], writes=[st_])
    p.tt(gv[:], gv[:], st_[:], ALU.mult, reads=[gv, st_], writes=[gv])
    p.tt(xt[:], xt[:], gv[:], ALU.add, reads=[xt, gv], writes=[xt])
    p.dma(io["x_out"].ap()[r, :], xt[:], reads=[xt], q="po")
    if layer == 3:
        return
    ln_tok(p, xn[:], xt[:], col, sq, 1e-5, rkeys=[xt], wkeys=[xn])
    p.dma(io["xn_out"].ap()[r, :], xn[:], reads=[xn], q="po")
    for j in range(8):
        cs = slice(j * 128, (j + 1) * 128)
        p.trans(ps_t[:], xn[:, cs], c["id32"][:], reads=[xn, c["id32"]],
                writes=[ps_t])
        p.copy(tb[:], ps_t[:], reads=[ps_t], writes=[tb])
        p.copy(tbb[:], ps_t[:], reads=[ps_t], writes=[tbb])
        p.dma(io["xnT32_out"].ap()[cs, r], tb[:], reads=[tb], q="po")
        p.dma(io["xnTb_out"].ap()[cs, r], tbb[:], reads=[tbb], q="po")


def emit_a1a(p, c, io):
    stg = p.sbuf("a1_stg", (128, 1024), F32)
    Wq = wload(p, "Wq", io["w_l1_Wq"], D, D, stage=stg)
    Wk = wload(p, "Wk", io["w_l1_Wk"], D, D, stage=stg)
    Wv = wload(p, "Wv", io["w_l1_Wv"], D, D, stage=stg)
    Wrk = wload(p, "Wrk", io["w_l1_Wrk"], D, D, stage=stg)
    Wrv = wload(p, "Wrv", io["w_l1_Wrv"], D, D, stage=stg)
    bq = bias_cols(p, "bq", io["w_l1_bq"], D)
    bk = bias_cols(p, "bk", io["w_l1_bk"], D)
    brk = bias_cols(p, "brk", io["w_l1_brk"], D)
    bvr = brow(p, "bvr", io["w_l1_bv"], D)
    brvr = brow(p, "brvr", io["w_l1_brv"], D)
    xb = [p.sbuf(f"a1_xb{i}", (128, 512), BF16) for i in range(8)]
    ps = p.psum("a1_ps", (128, 512), F32)
    ev = p.sbuf("a1_ev", (128, 512), BF16)
    for (wt, bcol, src, dst, TT_) in [
            (Wq, bq, "xn1Tb", "qT", T), (Wk, bk, "xn1Tb", "kT", T),
            (Wrk, brk, "pastTb0", "rkT0", SEQ),
            (Wrk, brk, "pastTb1", "rkT1", SEQ)]:
        CW = min(512, TT_)
        for tc in range(TT_ // CW):
            ts_ = slice(tc * CW, (tc + 1) * CW)
            for ci in range(8):
                p.dma(xb[ci][:, 0:CW],
                      io[src].ap()[ci * 128:(ci + 1) * 128, ts_],
                      writes=[xb[ci]])
            for dt_ in range(8):
                dsl = slice(dt_ * 128, (dt_ + 1) * 128)
                for ci in range(8):
                    p.mm(ps[:, 0:CW], wt[ci][:, dsl], xb[ci][:, 0:CW],
                         start=(ci == 0), stop=(ci == 7),
                         reads=[wt[ci], xb[ci]], writes=[ps])
                p.ts(ev[:, 0:CW], ps[:, 0:CW], bcol[:, dt_:dt_ + 1], ALU.add,
                     reads=[ps, bcol], writes=[ev])
                p.dma(io[dst].ap()[dsl, ts_], ev[:, 0:CW], reads=[ev], q="po")
    for (wt, brw, src, dst, TT_) in [
            (Wv, bvr, "xn1Tb", "v1", T), (Wrv, brvr, "pastTb0", "rv0", SEQ),
            (Wrv, brvr, "pastTb1", "rv1", SEQ)]:
        for it in range(TT_ // 128):
            r = slice(it * 128, (it + 1) * 128)
            for ci in range(8):
                p.dma(xb[ci][:, 0:128],
                      io[src].ap()[ci * 128:(ci + 1) * 128, r], writes=[xb[ci]])
            for dc in range(2):
                dsl = slice(dc * 512, (dc + 1) * 512)
                for ci in range(8):
                    p.mm(ps[:], xb[ci][:, 0:128], wt[ci][:, dsl],
                         start=(ci == 0), stop=False,
                         reads=[xb[ci], wt[ci]], writes=[ps])
                brb = p.sbuf(f"a1_brb_{dst}_{dc}", (1, 512), BF16)                     if it == 0 else None
                if it == 0:
                    p.copy(brb[:], brw[0:1, dsl], reads=[brw], writes=[brb])
                    setattr(emit_a1a, f"_brb_{dst}_{dc}", brb)
                brb = getattr(emit_a1a, f"_brb_{dst}_{dc}")
                p.mm(ps[:], c["ones_bf"][0:1, :], brb[0:1, :], start=False,
                     stop=True, reads=[brb, c["ones_bf"]], writes=[ps])
                p.copy(ev[:], ps[:], reads=[ps], writes=[ev])
                p.dma(io[dst].ap()[r, dsl], ev[:], reads=[ev], q="po")


def _mha(p, c, io, qT_name, kT_name, v_name, b, nh, skv, ctxT_sb, scale,
         accum, tag, kvoff=0):
    qoff = b * SEQ
    HW = min(1024, SEQ, skv)
    if not hasattr(p, "_mha_cache"):
        p._mha_cache = {}
    if tag not in p._mha_cache:
        p._mha_cache[tag] = (
            p.psum(f"mh_ps{tag}", (128, HW), F32),
            p.psum(f"mh_psc{tag}", (64, HW), F32),
            p.sbuf(f"mh_kt{tag}", (64, skv), BF16),
            p.sbuf(f"mh_qt{tag}", (64, SEQ), BF16),
            p.sbuf(f"mh_vt{tag}", (128, 64), BF16),
            p.sbuf(f"mh_m{tag}", (128, 8), F32),
            p.sbuf(f"mh_mb{tag}", (128, 1), BF16),
            p.sbuf(f"mh_den{tag}", (128, 2), F32),
            p.sbuf(f"mh_scr{tag}", (128, HW), BF16),
            p.sbuf(f"mh_rv2{tag}", (128, 16), F32),
            p.sbuf(f"mh_rvT{tag}", (16, 128), F32),
            p.sbuf(f"mh_rrow{tag}", (1, SEQ), F32))
    (ps_s, ps_c, kt, qt, vt, mcol, mb16, den, scr, rv2, rvT,
     rrow) = p._mha_cache[tag]
    nhalf = skv // HW
    for h in range(nh):
        hs = slice(h * 64, (h + 1) * 64)
        p.memset(rv2[:], 0.0, writes=[rv2])
        p.dma(kt[:], io[kT_name].ap()[hs, kvoff:kvoff + skv], writes=[kt])
        p.dma(qt[:], io[qT_name].ap()[hs, qoff:qoff + SEQ], writes=[qt])
        for st_ in range(SEQ // 128):
            ssl = slice(st_ * 128, (st_ + 1) * 128)
            for hf in range(nhalf):
                for off in range(0, HW, 512):
                    ww = min(512, HW - off)
                    p.mm(ps_s[:, off:off + ww], qt[:, ssl],
                         kt[:, hf * HW + off:hf * HW + off + ww],
                         start=True, stop=True, reads=[qt, kt], writes=[ps_s])
                p.red(mcol[:, hf:hf + 1], ps_s[:, 0:HW], ALU.max,
                      reads=[ps_s], writes=[mcol])
            if nhalf == 2:
                p.tt(mcol[:, 0:1], mcol[:, 0:1], mcol[:, 1:2], ALU.max,
                     reads=[mcol], writes=[mcol])
            p.copy(mb16[:], mcol[:, 0:1], reads=[mcol], writes=[mb16])
            p.copy(mcol[:, 2:3], mb16[:], reads=[mb16], writes=[mcol])
            p.ts(mcol[:, 3:4], mcol[:, 2:3], -0.125, ALU.mult, reads=[mcol],
                 writes=[mcol])
            for hf in range(nhalf):
                for off in range(0, HW, 512):
                    ww = min(512, HW - off)
                    p.mm(ps_s[:, off:off + ww], qt[:, ssl],
                         kt[:, hf * HW + off:hf * HW + off + ww],
                         start=True, stop=True, reads=[qt, kt], writes=[ps_s])
                p.act(scr[:, 0:HW], ps_s[:, 0:HW], AF.Exp, scale=0.125,
                      bias=mcol[:, 3:4], accum_out=den[:, hf:hf + 1],
                      reads=[ps_s, mcol], writes=[scr, den])
            if nhalf == 2:
                p.tt(den[:, 0:1], den[:, 0:1], den[:, 1:2], ALU.add,
                     reads=[den], writes=[den])
            p.act(den[:, 0:1], den[:, 0:1], AF.Ln, reads=[den], writes=[den])
            p.ts(den[:, 0:1], den[:, 0:1], 8.0, ALU.mult, reads=[den],
                 writes=[den])
            p.tt(rv2[:, st_:st_ + 1], mcol[:, 2:3], den[:, 0:1], ALU.add,
                 reads=[mcol, den], writes=[rv2])
        p.ts(rv2[:], rv2[:], -1.0, ALU.mult, reads=[rv2], writes=[rv2])
        p.trans(ps_s[0:16, 0:128], rv2[:, 0:16], c["id32"][:],
                reads=[rv2, c["id32"]], writes=[ps_s])
        p.copy(rvT[:], ps_s[0:16, 0:128], reads=[ps_s], writes=[rvT])
        p.dma(io["rowscr"].ap()[0:16, 0:128], rvT[:], reads=[rvT], q="po")
        p.dma(rrow[:], io["rowscr"].ap().rearrange("a b -> (a b)")
              [0:SEQ].unsqueeze(0), writes=[rrow])
        for sh in range(SEQ // HW):
            s0 = sh * HW
            for tt_ in range(skv // 128):
                tsl = slice(tt_ * 128, (tt_ + 1) * 128)
                for off in range(0, HW, 512):
                    ww = min(512, HW - off)
                    p.mm(ps_s[:, off:off + ww], kt[:, tsl],
                         qt[:, s0 + off:s0 + off + ww], start=True,
                         stop=False, reads=[kt, qt], writes=[ps_s])
                    p.mm(ps_s[:, off:off + ww], c["ones_f"][0:1, :],
                         rrow[0:1, s0 + off:s0 + off + ww], start=False,
                         stop=True, reads=[rrow, c["ones_f"]], writes=[ps_s])
                p.act(scr[:, 0:HW], ps_s[:, 0:HW], AF.Exp, scale=0.125,
                      reads=[ps_s], writes=[scr])
                p.dma(vt[:], io[v_name].ap()[kvoff + tt_ * 128:
                                             kvoff + (tt_ + 1) * 128, hs],
                      writes=[vt])
                for off in range(0, HW, 512):
                    ww = min(512, HW - off)
                    p.mm(ps_c[:, off:off + ww], vt[:], scr[:, off:off + ww],
                         start=(tt_ == 0), stop=(tt_ == skv // 128 - 1),
                         reads=[vt, scr], writes=[ps_c])
            dst = ctxT_sb[h // 2]
            rsl = slice(64 * (h % 2), 64 * (h % 2) + 64)
            if accum == 0:
                p.ts(dst[rsl, s0:s0 + HW], ps_c[:, 0:HW], scale, ALU.mult,
                     reads=[ps_c], writes=[dst])
            else:
                p.stt(dst[rsl, s0:s0 + HW], ps_c[:, 0:HW], scale,
                      dst[rsl, s0:s0 + HW], ALU.mult, ALU.add,
                      reads=[ps_c, dst], writes=[dst])


def emit_a1b(p, c, io):
    stg = p.sbuf("a1b_stg", (128, 1024), F32)
    Wo = wload(p, "Wo", io["w_l1_Wo"], D, D, stage=stg)
    bor = brow(p, "bor", io["w_l1_bo"], D)
    ctxT = [p.sbuf(f"a1b_ctxT{i}", (128, SEQ), BF16) for i in range(8)]
    ctxb = ctxT
    ps = p.psum("a1b_pso", (128, 512), F32)
    ev = p.sbuf("a1b_ev", (128, 1024), F32)
    borb = p.sbuf("a1b_borb", (1, 1024), BF16)
    p.copy(borb[:], bor[:], reads=[bor], writes=[borb])
    for b in range(BPC):
        _mha(p, c, io, "qT", "kT", "v1", b, 16, SEQ, ctxT, 1.0, 0, "sa",
             kvoff=b * SEQ)
        for it in range(SEQ // 128):
            r = slice(b * SEQ + it * 128, b * SEQ + (it + 1) * 128)
            rl = slice(it * 128, (it + 1) * 128)
            for dc in range(2):
                dsl = slice(dc * 512, (dc + 1) * 512)
                for ci in range(8):
                    p.mm(ps[:], ctxb[ci][:, rl], Wo[ci][:, dsl],
                         start=(ci == 0), stop=False,
                         reads=[ctxb[ci], Wo[ci]], writes=[ps])
                p.mm(ps[:], c["ones_bf"][0:1, :], borb[0:1, dsl],
                     start=False, stop=True, reads=[borb, c["ones_bf"]],
                     writes=[ps])
                p.copy(ev[:, dsl], ps[:], reads=[ps], writes=[ev])
            p.dma(io["attout"].ap()[r, :], ev[:], reads=[ev], q="po")


def emit_a1c(p, c, io):
    stg = p.sbuf("a1c_stg", (128, 1024), F32)
    Wrq = wload(p, "Wrq", io["w_l1_Wrq"], D, D, stage=stg)
    Wg1 = wload(p, "Wg1", io["w_l1_Wg1"], D, D, stage=stg)
    Wg2 = wload(p, "Wg2", io["w_l1_Wg2"], D, D, stage=stg)
    brq = bias_cols(p, "brq", io["w_l1_brq"], D)
    bgr = brow(p, "bgr", io["w_l1_bg"], D)
    aob = p.sbuf("a1c_aob", (128, 1024), BF16)
    ao = p.sbuf("a1c_ao", (128, 1024), F32)
    ps_t = p.psum("a1c_pst", (128, 128), BF16)
    aoT = [p.sbuf(f"a1c_aoT{i}", (128, SEQ), BF16) for i in range(8)]
    ps = p.psum("a1c_ps", (128, 512), F32)
    ev = p.sbuf("a1c_ev", (128, 512), BF16)
    ctxT = [p.sbuf(f"a1c_ctxT{i}", (128, SEQ), BF16) for i in range(8)]
    ctxb = ctxT
    gt = p.sbuf("a1c_gt", (128, 1024), F32)
    rc = p.sbuf("a1c_rc", (128, 1024), F32)
    ps_t2 = p.psum("a1c_pst2", (128, 128), BF16)
    sv = p.sbuf("a1c_sv", (128, 1024), F32)
    for b in range(BPC):
        for it in range(SEQ // 128):
            r = slice(b * SEQ + it * 128, b * SEQ + (it + 1) * 128)
            p.dma(ao[:], io["attout"].ap()[r, :], writes=[ao])
            p.copy(aob[:], ao[:], reads=[ao], writes=[aob])
            for j in range(8):
                p.trans(ps_t[:], aob[:, j * 128:(j + 1) * 128], c["idbf"][:],
                        reads=[aob, c["idbf"]], writes=[ps_t])
                p.copy(aoT[j][:, it * 128:(it + 1) * 128], ps_t[:],
                       reads=[ps_t], writes=[aoT[j]])
        CW2 = min(512, SEQ)
        for tc in range(SEQ // CW2):
            ts_ = slice(tc * CW2, (tc + 1) * CW2)
            for dt_ in range(8):
                dsl = slice(dt_ * 128, (dt_ + 1) * 128)
                for ci in range(8):
                    p.mm(ps[:, 0:CW2], Wrq[ci][:, dsl], aoT[ci][:, ts_],
                         start=(ci == 0), stop=(ci == 7),
                         reads=[Wrq[ci], aoT[ci]], writes=[ps])
                p.ts(ev[:, 0:CW2], ps[:, 0:CW2], brq[:, dt_:dt_ + 1],
                     ALU.add, reads=[ps, brq], writes=[ev])
                p.dma(io["rqT"].ap()[dsl, ts_], ev[:, 0:CW2], reads=[ev],
                      q="po")
        _mha(p, c, io, "rqT", "rkT0", "rv0", 0, 16, SEQ, ctxT, 0.5, 0, "ra")
        _mha(p, c, io, "rqT", "rkT1", "rv1", 0, 16, SEQ, ctxT, 0.5, 1, "ra")
        for it in range(SEQ // 128):
            r = slice(b * SEQ + it * 128, b * SEQ + (it + 1) * 128)
            rl = slice(it * 128, (it + 1) * 128)
            for dc in range(2):
                dsl = slice(dc * 512, (dc + 1) * 512)
                for ci in range(8):
                    p.mm(ps[:], aoT[ci][:, rl], Wg1[ci][:, dsl],
                         start=(ci == 0), stop=False,
                         reads=[aoT[ci], Wg1[ci]], writes=[ps])
                for ci in range(8):
                    p.mm(ps[:], ctxb[ci][:, rl], Wg2[ci][:, dsl],
                         start=False, stop=False,
                         reads=[ctxb[ci], Wg2[ci]], writes=[ps])
                p.mm(ps[:], c["ones_f"][0:1, :], bgr[0:1, dsl], start=False,
                     stop=True, reads=[bgr, c["ones_f"]], writes=[ps])
                p.act(gt[:, dsl], ps[:], AF.Sigmoid, reads=[ps], writes=[gt])
            for j in range(8):
                p.trans(ps_t2[:], ctxT[j][:, rl], c["idbf"][:],
                        reads=[ctxT[j], c["idbf"]], writes=[ps_t2])
                p.copy(rc[:, j * 128:(j + 1) * 128], ps_t2[:],
                       reads=[ps_t2], writes=[rc])
            p.dma(ao[:], io["attout"].ap()[r, :], writes=[ao])
            p.tt(rc[:], rc[:], ao[:], ALU.subtract, reads=[rc, ao], writes=[rc])
            p.tt(rc[:], rc[:], gt[:], ALU.mult, reads=[rc, gt], writes=[rc])
            p.tt(sv[:], ao[:], rc[:], ALU.add, reads=[ao, rc], writes=[sv])
            p.dma(io["s_out"].ap()[r, :], sv[:], reads=[sv], q="po")


def emit_moe(p, c, io, nexp, ffdim, l3=False):
    router = wload(p, "router", io["w_router"], D, nexp, out_dt=F32)
    ps_r = p.psum("moe_psr", (128, max(nexp, 8)), F32)
    lg = p.sbuf("moe_lg", (128, nexp), F32)
    we = p.sbuf("moe_we", (128, nexp), F32)
    colr = p.sbuf("moe_colr", (128, 8), F32)
    xb32 = [p.sbuf(f"moe_x32{i}", (128, 128), F32) for i in range(8)]
    xb = [p.sbuf(f"moe_xb{i}", (128, 256), BF16) for i in range(8)]
    ps_h = p.psum("moe_psh", (128, 512), F32)
    ps_h2 = p.psum("moe_psh2", (128, 512), F32)
    F2 = ffdim // 128
    hT = [p.sbuf(f"moe_hT{i}", (128, 256), BF16) for i in range(F2)]
    ps_o = p.psum("moe_pso", (128, 1024), F32)
    acc = p.sbuf("moe_acc", (128, 1024), F32)
    gv = p.sbuf("moe_gv", (128, 1024), F32)
    sw1 = p.sbuf("moe_sw1", (128, 512), F32)
    sw2 = p.sbuf("moe_sw2", (128, 512), F32)
    stg = p.sbuf("moe_stg", (128, 2048), F32)
    bufs = (p.sbuf("moe_bxt", (128, 1024), F32),
            p.sbuf("moe_bst", (128, 1024), F32),
            p.sbuf("moe_bxn", (128, 1024), F32),
            p.sbuf("moe_bcol", (128, 8), F32),
            p.sbuf("moe_bsq", (128, 1024), F32),
            p.psum("moe_bpst", (128, 128), F32),
            p.sbuf("moe_btb", (128, 128), F32),
            p.sbuf("moe_btbb", (128, 128), BF16))
    for it in range(T // 128):
        r = slice(it * 128, (it + 1) * 128)
        for ci in range(8):
            p.dma(xb32[ci][:], io["xnT32_in"].ap()[ci * 128:(ci + 1) * 128, r],
                  writes=[xb32[ci]])
        for ci in range(8):
            p.mm(ps_r[:, 0:nexp], xb32[ci][:], router[ci][:, :],
                 start=(ci == 0), stop=(ci == 7),
                 reads=[xb32[ci], router[ci]], writes=[ps_r])
        p.copy(lg[:], ps_r[:, 0:nexp], reads=[ps_r], writes=[lg])
        m1, m2, w1c, w2c = (colr[:, 0:1], colr[:, 1:2], colr[:, 2:3],
                            colr[:, 3:4])
        p.red(m1, lg[:], ALU.max, reads=[lg], writes=[colr])
        p.ts(we[:], lg[:], m1, ALU.is_equal, reads=[lg, colr], writes=[we])
        p.stt(lg[:], we[:], -1e30, lg[:], ALU.mult, ALU.add,
              reads=[we, lg], writes=[lg])
        p.red(m2, lg[:], ALU.max, reads=[lg], writes=[colr])
        p.tt(colr[:, 4:5], m1, m2, ALU.subtract, reads=[colr], writes=[colr])
        p.act(w1c, colr[:, 4:5], AF.Sigmoid, reads=[colr], writes=[colr])
        p.act(w2c, w1c, AF.Identity, scale=-1.0, bias=1.0, reads=[colr],
              writes=[colr])
        p.ts(we[:], we[:], w1c, ALU.mult, reads=[we, colr], writes=[we])
        p.ts(lg[:], lg[:], m2, ALU.is_equal, reads=[lg, colr], writes=[lg])
        p.stt(we[:], lg[:], w2c, we[:], ALU.mult, ALU.add,
              reads=[lg, colr, we], writes=[we])
        p.dma(io["weS"].ap()[r, :], we[:, 0:nexp], reads=[we], q="po")
    C1 = 2 * ffdim if l3 else ffdim
    w1 = [p.sbuf(f"moe_w1_{i}", (128, C1), BF16) for i in range(8)]
    w2 = [p.sbuf(f"moe_w2_{i}", (128, D), BF16) for i in range(F2)]
    b1c = p.sbuf("moe_b1c", (128, C1 // 128), F32)
    b2r32 = p.sbuf("moe_b2r32", (1, D), F32)
    b2r = p.sbuf("moe_b2r", (1, D), BF16)
    TCH = 256
    for e in range(nexp):
        for i in range(8):
            for hf_ in range((C1 + 2047) // 2048):
                wd = min(2048, C1 - hf_ * 2048)
                p.dma(stg[:, 0:wd], io[f"w_e1_{e}"].ap()
                      [i * 128:(i + 1) * 128, hf_ * 2048:hf_ * 2048 + wd],
                      writes=[stg])
                p.copy(w1[i][:, hf_ * 2048:hf_ * 2048 + wd], stg[:, 0:wd],
                       reads=[stg], writes=[w1[i]])
        for i in range(F2):
            p.dma(stg[:, 0:D], io[f"w_e2_{e}"].ap()[i * 128:(i + 1) * 128, :],
                  writes=[stg])
            p.copy(w2[i][:], stg[:, 0:D], reads=[stg], writes=[w2[i]])
        for j in range(C1 // 128):
            p.dma(b1c[:, j:j + 1],
                  io[f"w_eb1_{e}"].ap()[j * 128:(j + 1) * 128].unsqueeze(-1),
                  writes=[b1c])
        p.dma(b2r32[:], io[f"w_eb2_{e}"].ap()[:].unsqueeze(0), writes=[b2r32])
        p.copy(b2r[:], b2r32[:], reads=[b2r32], writes=[b2r])
        for tc in range(T // TCH):
            ts_ = slice(tc * TCH, (tc + 1) * TCH)
            for ci in range(8):
                p.dma(xb[ci][:, 0:TCH],
                      io["xnTb_in"].ap()[ci * 128:(ci + 1) * 128, ts_],
                      writes=[xb[ci]])
            for ft in range(F2):
                fsl0 = slice(ft * 128, (ft + 1) * 128)
                for ci in range(8):
                    p.mm(ps_h[:, 0:TCH], w1[ci][:, fsl0], xb[ci][:, 0:TCH],
                         start=(ci == 0), stop=(ci == 7),
                         reads=[w1[ci], xb[ci]], writes=[ps_h])
                if not l3:
                    p.act(hT[ft][:, 0:TCH], ps_h[:, 0:TCH], AF.Gelu,
                          bias=b1c[:, ft:ft + 1], reads=[ps_h, b1c],
                          writes=[hT[ft]])
                else:
                    fsl1 = slice(ffdim + ft * 128, ffdim + (ft + 1) * 128)
                    for ci in range(8):
                        p.mm(ps_h2[:, 0:TCH], w1[ci][:, fsl1],
                             xb[ci][:, 0:TCH], start=(ci == 0),
                             stop=(ci == 7), reads=[w1[ci], xb[ci]],
                             writes=[ps_h2])
                    p.ts(sw1[:, 0:TCH], ps_h[:, 0:TCH], b1c[:, ft:ft + 1],
                         ALU.add, 9.0, ALU.min, reads=[ps_h, b1c],
                         writes=[sw1])
                    p.ts(sw2[:, 0:TCH], ps_h2[:, 0:TCH],
                         b1c[:, F2 + ft:F2 + ft + 1], ALU.add, 9.0,
                         ALU.min, reads=[ps_h2, b1c], writes=[sw2])
                    p.ts(sw2[:, 0:TCH], sw2[:, 0:TCH], -9.0, ALU.max, 1.0,
                         ALU.add, reads=[sw2], writes=[sw2])
                    p.tt(sw2[:, 0:TCH], sw2[:, 0:TCH], sw1[:, 0:TCH],
                         ALU.mult, reads=[sw1, sw2], writes=[sw2])
                    p.act(sw1[:, 0:TCH], sw1[:, 0:TCH], AF.Sigmoid,
                          scale=1.702, reads=[sw1], writes=[sw1])
                    p.tt(hT[ft][:, 0:TCH], sw1[:, 0:TCH], sw2[:, 0:TCH],
                         ALU.mult, reads=[sw1, sw2], writes=[hT[ft]])
            for sub in range(TCH // 128):
                r = slice(tc * TCH + sub * 128, tc * TCH + (sub + 1) * 128)
                rl = slice(sub * 128, (sub + 1) * 128)
                for ft in range(F2):
                    mmS(p, ps_o, slice(None), 0, hT[ft][:, rl], w2[ft],
                        slice(None), 0, D, ft == 0, False,
                        [hT[ft], w2[ft]], [ps_o])
                rank1S(p, ps_o, slice(None), 0, c["ones_bf"], b2r, 0, D,
                       False, True, [b2r], [ps_o])
                p.dma(we[:, 0:nexp], io["weS"].ap()[r, :], writes=[we])
                if e > 0:
                    p.dma(acc[:], io["accS"].ap()[r, :], writes=[acc])
                    p.stt(acc[:], ps_o[:], we[:, e:e + 1], acc[:],
                          ALU.mult, ALU.add, reads=[ps_o, we, acc],
                          writes=[acc])
                else:
                    p.ts(acc[:], ps_o[:], we[:, 0:1], ALU.mult,
                         reads=[ps_o, we], writes=[acc])
                if e < nexp - 1:
                    p.dma(io["accS"].ap()[r, :], acc[:], reads=[acc], q="po")
                else:
                    p.act(gv[:], acc[:], AF.Silu, reads=[acc], writes=[gv])
                    glue_tail(p, c, io, r, gv, bufs, 3 if l3 else 1)



def emit_a2(p, c, io):
    stg = p.sbuf("a2_stg", (128, 1024), F32)
    Wqt = wload(p, "a2Wq", io["w_l2_qwT"], D, D, stage=stg)
    Wkt = wload(p, "a2Wk", io["w_l2_kwT"], D, D, stage=stg)
    Wvt = wload(p, "a2Wv", io["w_l2_vwT"], D, D, stage=stg)
    Wot = wload(p, "a2Wo", io["w_l2_owT"], D, D, stage=stg)
    qb = bias_cols(p, "a2qb", io["w_l2_qb"], D)
    kb = bias_cols(p, "a2kb", io["w_l2_kb"], D)
    vbr = brow(p, "a2vb", io["w_l2_vb"], D, out_dt=BF16)
    obr = brow(p, "a2ob", io["w_l2_ob"], D, out_dt=BF16)
    band = p.sbuf("a2_band", (128, 192), F32)
    p.dma(band[:], io["band"].ap()[:, :], writes=[band])
    xb = [p.sbuf(f"a2_xb{i}", (128, 512), BF16) for i in range(8)]
    ps = p.psum("a2_ps", (128, 512), F32)
    ev = p.sbuf("a2_ev", (128, 512), BF16)
    CW = 512
    for (wt, bcol, dst) in [(Wqt, qb, "qT"), (Wkt, kb, "kT")]:
        for tc in range(T // 512):
            ts_ = slice(tc * 512, (tc + 1) * 512)
            for ci in range(8):
                p.dma(xb[ci][:], io["xn2Tb"].ap()[ci * 128:(ci + 1) * 128, ts_],
                      writes=[xb[ci]])
            for dt_ in range(8):
                dsl = slice(dt_ * 128, (dt_ + 1) * 128)
                for ci in range(8):
                    p.mm(ps[:, 0:CW], wt[ci][:, dsl], xb[ci][:, 0:CW],
                         start=(ci == 0), stop=(ci == 7),
                         reads=[wt[ci], xb[ci]], writes=[ps])
                p.ts(ev[:, 0:CW], ps[:, 0:CW], bcol[:, dt_:dt_ + 1], ALU.add,
                     reads=[ps, bcol], writes=[ev])
                p.dma(io[dst].ap()[dsl, ts_], ev[:, 0:CW], reads=[ev], q="po")
    for it in range(T // 128):
        r = slice(it * 128, (it + 1) * 128)
        for ci in range(8):
            p.dma(xb[ci][:, 0:128], io["xn2Tb"].ap()[ci * 128:(ci + 1) * 128, r],
                  writes=[xb[ci]])
        for dc in range(2):
            dsl = slice(dc * 512, (dc + 1) * 512)
            for ci in range(8):
                p.mm(ps[:], xb[ci][:, 0:128], Wvt[ci][:, dsl], start=(ci == 0),
                     stop=False, reads=[xb[ci], Wvt[ci]], writes=[ps])
            p.mm(ps[:], c["ones_bf"][0:1, :], vbr[0:1, dsl], start=False,
                 stop=True, reads=[vbr, c["ones_bf"]], writes=[ps])
            p.copy(ev[:], ps[:], reads=[ps], writes=[ev])
            p.dma(io["v1"].ap()[r, dsl], ev[:], reads=[ev], q="po")
    kth = p.sbuf("a2_kth", (64, SEQ + 64), BF16)
    vpad = [p.sbuf(f"a2_vp{i}", (128, 1024), BF16) for i in range(SEQ // 128 + 1)]
    qt = p.sbuf("a2_qt", (64, SEQ), BF16)
    ps_s = p.psum("a2_pss", (128, 192), F32)
    sc = p.sbuf("a2_sc", (128, 192), F32)
    mcol = p.sbuf("a2_mcol", (128, 4), F32)
    att = p.sbuf("a2_att", (128, 192), BF16)
    ps_t = p.psum("a2_pst", (128, 128), BF16)
    a1_ = p.sbuf("a2_a1", (128, 128), BF16)
    a2_ = p.sbuf("a2_a2", (64, 128), BF16)
    ps_c = p.psum("a2_psc", (64, 128), F32)
    ctxT = [p.sbuf(f"a2_ctxT{i}", (128, SEQ), BF16) for i in range(8)]
    ov = p.sbuf("a2_ov", (128, 512), F32)
    for b in range(BPC):
        for j in range(SEQ // 128 + 1):
            lo = j * 128 - 32 + b * SEQ
            rows = []
            p.memset(vpad[j][:], 0.0, writes=[vpad[j]])
            lo_c = max(lo, b * SEQ)
            hi_c = min(lo + 128, (b + 1) * SEQ)
            if hi_c > lo_c:
                p.dma(vpad[j][lo_c - lo:hi_c - lo, :],
                      io["v1"].ap()[lo_c:hi_c, :], writes=[vpad[j]])
        for h in range(16):
            hs = slice(h * 64, (h + 1) * 64)
            p.dma(qt[:], io["qT"].ap()[hs, b * SEQ:(b + 1) * SEQ], writes=[qt])
            p.memset(kth[:, 0:32], 0.0, writes=[kth])
            p.memset(kth[:, SEQ + 32:SEQ + 64], 0.0, writes=[kth])
            p.dma(kth[:, 32:SEQ + 32],
                  io["kT"].ap()[hs, b * SEQ:(b + 1) * SEQ], writes=[kth])
            for st_ in range(SEQ // 128):
                ssl = slice(st_ * 128, (st_ + 1) * 128)
                p.mm(ps_s[:, 0:192], qt[:, ssl],
                     kth[:, st_ * 128:st_ * 128 + 192],
                     start=True, stop=True, reads=[qt, kth], writes=[ps_s])
                p.tt(sc[:], ps_s[:, 0:192], band[:], ALU.add,
                     reads=[ps_s, band], writes=[sc])
                p.red(mcol[:, 0:1], sc[:], ALU.max, reads=[sc], writes=[mcol])
                p.ts(mcol[:, 1:2], mcol[:, 0:1], -0.125, ALU.mult,
                     reads=[mcol], writes=[mcol])
                p.act(att[:], sc[:], AF.Exp, scale=0.125, bias=mcol[:, 1:2],
                      accum_out=mcol[:, 2:3], reads=[sc, mcol],
                      writes=[att, mcol])
                p.emit("dve", lambda e: e.reciprocal(mcol[:, 3:4],
                                                     mcol[:, 2:3]),
                       reads=[mcol], writes=[mcol])
                p.ts(att[:], att[:], mcol[:, 3:4], ALU.mult,
                     reads=[att, mcol], writes=[att])
                p.trans(ps_t[:], att[:, 0:128], c["idbf"][:],
                        reads=[att, c["idbf"]], writes=[ps_t])
                p.copy(a1_[:], ps_t[:], reads=[ps_t], writes=[a1_])
                p.trans(ps_t[0:64, :], att[:, 128:192], c["idbf"][:],
                        reads=[att, c["idbf"]], writes=[ps_t])
                p.copy(a2_[:], ps_t[0:64, :], reads=[ps_t], writes=[a2_])
                p.mm(ps_c[:], vpad[st_][:, hs], a1_[:], start=True,
                     stop=False, reads=[vpad[st_], a1_], writes=[ps_c])
                p.mm(ps_c[:], vpad[st_ + 1][0:64, hs], a2_[:], start=False,
                     stop=True, reads=[vpad[st_ + 1], a2_], writes=[ps_c])
                dst = ctxT[h // 2]
                p.copy(dst[64 * (h % 2):64 * (h % 2) + 64, ssl], ps_c[:],
                       reads=[ps_c], writes=[dst])
        for it in range(SEQ // 128):
            r = slice(b * SEQ + it * 128, b * SEQ + (it + 1) * 128)
            rl = slice(it * 128, (it + 1) * 128)
            for dc in range(2):
                dsl = slice(dc * 512, (dc + 1) * 512)
                for ci in range(8):
                    p.mm(ps[:], ctxT[ci][:, rl], Wot[ci][:, dsl],
                         start=(ci == 0), stop=False,
                         reads=[ctxT[ci], Wot[ci]], writes=[ps])
                p.mm(ps[:], c["ones_bf"][0:1, :], obr[0:1, dsl], start=False,
                     stop=True, reads=[obr, c["ones_bf"]], writes=[ps])
                p.copy(ov[:], ps[:], reads=[ps], writes=[ov])
                p.dma(io["s_out"].ap()[r, dsl], ov[:], reads=[ov], q="po")


def emit_b2a(p, c, io):
    stg = p.sbuf("b2a_stg", (128, 4096), F32)
    Wg = wload(p, "b2Wg", io["w_l2_Wg"], D, 4096, stage=stg)
    Wu = wload(p, "b2Wu", io["w_l2_Wu"], D, 4096, stage=stg)
    bgc = bias_cols(p, "b2bg", io["w_l2_bg"], 4096)
    buc = bias_cols(p, "b2bu", io["w_l2_bu"], 4096)
    xb = [p.sbuf(f"b2a_xb{i}", (128, 512), BF16) for i in range(8)]
    ps = p.psum("b2a_ps", (128, 512), F32)
    ps2 = p.psum("b2a_ps2", (128, 512), F32)
    uv = p.sbuf("b2a_uv", (128, 512), F32)
    ev = p.sbuf("b2a_ev", (128, 512), BF16)
    for tc in range(T // 512):
        ts_ = slice(tc * 512, (tc + 1) * 512)
        for ci in range(8):
            p.dma(xb[ci][:], io["xn2Tb"].ap()[ci * 128:(ci + 1) * 128, ts_],
                  writes=[xb[ci]])
        for dt_ in range(32):
            dsl = slice(dt_ * 128, (dt_ + 1) * 128)
            for ci in range(8):
                p.mm(ps[:], Wg[ci][:, dsl], xb[ci][:], start=(ci == 0),
                     stop=(ci == 7), reads=[Wg[ci], xb[ci]], writes=[ps])
            for ci in range(8):
                p.mm(ps2[:], Wu[ci][:, dsl], xb[ci][:], start=(ci == 0),
                     stop=(ci == 7), reads=[Wu[ci], xb[ci]], writes=[ps2])
            p.ts(uv[:], ps2[:], buc[:, dt_:dt_ + 1], ALU.add,
                 reads=[ps2, buc], writes=[uv])
            p.act(ev[:], ps[:], AF.Silu, bias=bgc[:, dt_:dt_ + 1],
                  reads=[ps, bgc], writes=[ev])
            p.tt(ev[:], ev[:], uv[:], ALU.mult, reads=[ev, uv], writes=[ev])
            p.dma(io["gT"].ap()[dsl, ts_], ev[:], reads=[ev], q="po")
    return


def emit_b2a2(p, c, io):
    stg = p.sbuf("b2a2_stg", (128, 1024), F32)
    ps = p.psum("b2a2_ps", (128, 512), F32)
    Wd = wload(p, "b2Wd", io["w_l2_Wd"], 4096, D, stage=stg)
    bdr = brow(p, "b2bd", io["w_l2_bd"], D, out_dt=BF16)
    pt = [p.sbuf(f"b2a_pt{i}", (128, 128), BF16) for i in range(32)]
    ov = p.sbuf("b2a_ov", (128, 512), F32)
    for it in range(T // 128):
        r = slice(it * 128, (it + 1) * 128)
        for ci in range(32):
            p.dma(pt[ci][:], io["gT"].ap()[ci * 128:(ci + 1) * 128, r],
                  writes=[pt[ci]])
        for dc in range(2):
            dsl = slice(dc * 512, (dc + 1) * 512)
            for ci in range(32):
                p.mm(ps[:], pt[ci][:], Wd[ci][:, dsl], start=(ci == 0),
                     stop=False, reads=[pt[ci], Wd[ci]], writes=[ps])
            p.mm(ps[:], c["ones_bf"][0:1, :], bdr[0:1, dsl], start=False,
                 stop=True, reads=[bdr, c["ones_bf"]], writes=[ps])
            p.copy(ov[:], ps[:], reads=[ps], writes=[ov])
            p.dma(io["mlp"].ap()[r, dsl], ov[:], reads=[ov], q="po")


def emit_b2b1(p, c, io):
    Wq = wload(p, "pWq", io["w_l2_Wq"], D, 512, out_dt=F32)
    keysT = p.sbuf("p_keysT", (128, 512), F32)
    p.dma(keysT[:], io["w_l2_keysT"].ap()[:, :], writes=[keysT])
    xb32 = [p.sbuf(f"p_x32{i}", (128, 512), F32) for i in range(8)]
    qTc = [p.sbuf(f"p_qT{i}", (128, 512), F32) for i in range(4)]
    ps_q = p.psum("p_psq", (128, 512), F32)
    sims = p.sbuf("p_sims", (128, 512), F32)
    simo = p.sbuf("p_simo", (128, 512), F32)
    m8 = p.sbuf("p_m8", (128, 8), F32)
    tv = p.sbuf("p_tv", (128, 64), F32)
    eq = p.sbuf("p_eq", (128, 512), F32)
    S64 = p.sbuf("p_S64", (128, 256), F32)
    t8 = p.sbuf("p_t8", (128, 32), F32)
    tmp32 = p.sbuf("p_tmp32", (128, 32), F32)
    m4 = p.sbuf("p_m4", (128, 4), F32)
    zc = p.sbuf("p_zc", (128, 4), F32)
    kp = p.sbuf("p_kp", (128, 4), F32)
    thr = p.sbuf("p_thr", (128, 4), F32)
    sxe = p.sbuf("p_sxe", (128, 256), F32)
    ps_t = p.psum("p_pst", (128, 128), F32)
    sxeT = [p.sbuf(f"p_sxeT{h}", (64, 512), F32) for h in range(4)]
    syT = [p.sbuf(f"p_syT{h}", (64, 512), F32) for h in range(4)]
    thrR = [p.sbuf(f"p_thrR{h}", (1, 512), F32) for h in range(4)]
    nthrR = [p.sbuf(f"p_nthrR{h}", (1, 512), F32) for h in range(4)]
    for tc in range(T // 512):
        ts_ = slice(tc * 512, (tc + 1) * 512)
        for ci in range(8):
            p.dma(xb32[ci][:], io["xn2T32"].ap()[ci * 128:(ci + 1) * 128, ts_],
                  writes=[xb32[ci]])
        for qt_ in range(4):
            for ci in range(8):
                p.mm(ps_q[:], Wq[ci][:, qt_ * 128:(qt_ + 1) * 128].transpose()
                     if False else Wq[ci][:, qt_ * 128:(qt_ + 1) * 128],
                     xb32[ci][:], start=(ci == 0), stop=(ci == 7),
                     reads=[Wq[ci], xb32[ci]], writes=[ps_q])
            p.copy(qTc[qt_][:], ps_q[:], reads=[ps_q], writes=[qTc[qt_]])
        for sub in range(4):
            ssl = slice(sub * 128, (sub + 1) * 128)
            for pp in range(2):
                for h in range(4):
                    cq0 = pp * 256 + h * 64
                    ti, rb = cq0 // 128, cq0 % 128
                    seg = pp * 4 + h
                    p.mm(ps_q[:, 0:64], qTc[ti][rb:rb + 64, ssl],
                         keysT[rb:rb + 64,
                               (h * 2 + pp) * 64:(h * 2 + pp) * 64 + 64],
                         start=True, stop=True, reads=[qTc[ti], keysT],
                         writes=[ps_q])
                    p.copy(sims[:, seg * 64:(seg + 1) * 64], ps_q[:, 0:64],
                           reads=[ps_q], writes=[sims])
            p.copy(simo[:], sims[:], reads=[sims], writes=[simo])
            s3 = sims[:].rearrange("p (s j) -> p s j", s=8)
            for rnk in range(8):
                p.red(m8[:], s3, ALU.max, reads=[sims], writes=[m8])
                p.copy(tv[:].rearrange("p (s j) -> p s j", s=8)[:, :, rnk:rnk + 1],
                       m8[:].unsqueeze(-1), reads=[m8], writes=[tv])
                p.tt(eq[:].rearrange("p (s j) -> p s j", s=8), s3,
                     m8[:].unsqueeze(-1).broadcast_to((128, 8, 64)),
                     ALU.is_equal, reads=[sims, m8], writes=[eq])
                p.stt(sims[:], eq[:], -1e30, sims[:], ALU.mult, ALU.add,
                      reads=[eq, sims], writes=[sims])
            tv3 = tv[:].rearrange("p (s j) -> p s j", s=8)
            for h in range(4):
                p.tt(S64[:, h * 64:(h + 1) * 64].rearrange(
                    "p (i j) -> p i j", i=8),
                    tv3[:, h:h + 1, :].rearrange("p a b -> p (a b)")
                    .unsqueeze(-1).broadcast_to((128, 8, 8)),
                    tv3[:, 4 + h:5 + h, :].rearrange("p a b -> p (a b)")
                    .unsqueeze(1).broadcast_to((128, 8, 8)),
                    ALU.add, reads=[tv], writes=[S64])
            S3 = S64[:].rearrange("p (s j) -> p s j", s=4)
            for rnk in range(8):
                p.red(m4[:], S3, ALU.max, reads=[S64], writes=[m4])
                p.copy(t8[:].rearrange("p (s j) -> p s j", s=4)
                       [:, :, rnk:rnk + 1], m4[:].unsqueeze(-1),
                       reads=[m4], writes=[t8])
                p.tt(eq[:, 0:256].rearrange("p (s j) -> p s j", s=4), S3,
                     m4[:].unsqueeze(-1).broadcast_to((128, 4, 64)),
                     ALU.is_equal, reads=[S64, m4], writes=[eq])
                p.stt(S64[:], eq[:, 0:256], -1e30, S64[:], ALU.mult, ALU.add,
                      reads=[eq, S64], writes=[S64])
            t83 = t8[:].rearrange("p (s j) -> p s j", s=4)
            p.tt(tmp32[:].rearrange("p (s j) -> p s j", s=4), t83,
                 t8[:].rearrange("p (s j) -> p s j", s=4)[:, :, 0:1]
                 .broadcast_to((128, 4, 8)), ALU.subtract,
                 reads=[t8], writes=[tmp32])
            p.act(tmp32[:], tmp32[:], AF.Exp, reads=[tmp32], writes=[tmp32])
            p.red(zc[:], tmp32[:].rearrange("p (s j) -> p s j", s=4),
                  ALU.add, reads=[tmp32], writes=[zc])
            p.act(zc[:], zc[:], AF.Ln, reads=[zc], writes=[zc])
            p.tt(kp[:], t83[:, :, 0:1].rearrange("p a b -> p (a b)"), zc[:],
                 ALU.add, reads=[t8, zc], writes=[kp])
            p.ts(kp[:], kp[:], -1.0, ALU.mult, reads=[kp], writes=[kp])
            p.tt(thr[:], t83[:, :, 7:8].rearrange("p a b -> p (a b)"), kp[:],
                 ALU.add, reads=[t8, kp], writes=[thr])
            p.tt(sxe[:].rearrange("p (s j) -> p s j", s=4),
                 simo[:].rearrange("p (s j) -> p s j", s=8)[:, 0:4, :],
                 kp[:].unsqueeze(-1).broadcast_to((128, 4, 64)), ALU.add,
                 reads=[simo, kp], writes=[sxe])
            for h in range(4):
                p.trans(ps_t[0:64, :], sxe[:, h * 64:(h + 1) * 64],
                        c["id32"][:], reads=[sxe, c["id32"]], writes=[ps_t])
                p.copy(sxeT[h][:, ssl], ps_t[0:64, :], reads=[ps_t],
                       writes=[sxeT[h]])
                p.trans(ps_t[0:64, :], simo[:, 256 + h * 64:256 + (h + 1) * 64],
                        c["id32"][:], reads=[simo, c["id32"]], writes=[ps_t])
                p.copy(syT[h][:, ssl], ps_t[0:64, :], reads=[ps_t],
                       writes=[syT[h]])
                p.trans(ps_t[0:1, :], thr[:, h:h + 1], c["id32"][:],
                        reads=[thr, c["id32"]], writes=[ps_t])
                p.copy(thrR[h][0:1, ssl], ps_t[0:1, :], reads=[ps_t],
                       writes=[thrR[h]])
        for h in range(4):
            p.dma(io["sxeTS"].ap()[h * 64:(h + 1) * 64, ts_], sxeT[h][:],
                  reads=[sxeT[h]], q="po")
            p.dma(io["syTS"].ap()[h * 64:(h + 1) * 64, ts_], syT[h][:],
                  reads=[syT[h]], q="po")
            p.dma(io["thrRS"].ap()[h:h + 1, ts_], thrR[h][:],
                  reads=[thrR[h]], q="po")


def emit_b2b1d(p, c, io):
    stg = p.sbuf("pd_stg", (128, 4096), F32)
    deT = wload(p, "pd_deT", io["w_l2_deT"], D, 4096, stage=stg)
    R64 = wload(p, "pd_R64", io["w_l2_R64"], 64, 4096, out_dt=F32)
    T64 = wload(p, "pd_T64", io["w_l2_T64"], 64, 4096, out_dt=F32)
    xb = [p.sbuf(f"pd_xb{i}", (128, 512), BF16) for i in range(8)]
    sxeT = [p.sbuf(f"pd_sxeT{h}", (64, 512), F32) for h in range(4)]
    syT = [p.sbuf(f"pd_syT{h}", (64, 512), F32) for h in range(4)]
    nthrR = [p.sbuf(f"pd_nthrR{h}", (1, 512), F32) for h in range(4)]
    ps_d = p.psum("pd_psd", (128, 512), F32)
    ps_m = p.psum("pd_psm", (128, 512), F32)
    P_sb = p.sbuf("pd_P", (128, 512), F32)
    M01 = p.sbuf("pd_M01", (128, 512), F32)
    Mtmp = p.sbuf("pd_Mtmp", (128, 512), F32)
    Macc = p.sbuf("pd_Macc", (128, 512), F32)
    ps_w = p.psum("pd_psw", (128, 512), F32)
    wsil = p.sbuf("pd_wsil", (128, 512), F32)
    mcb = p.sbuf("pd_mcb", (128, 512), BF16)
    for tc in range(T // 512):
        ts_ = slice(tc * 512, (tc + 1) * 512)
        for ci in range(8):
            p.dma(xb[ci][:], io["xn2Tb"].ap()[ci * 128:(ci + 1) * 128, ts_],
                  writes=[xb[ci]])
        for h in range(4):
            p.dma(sxeT[h][:], io["sxeTS"].ap()[h * 64:(h + 1) * 64, ts_],
                  writes=[sxeT[h]])
            p.dma(syT[h][:], io["syTS"].ap()[h * 64:(h + 1) * 64, ts_],
                  writes=[syT[h]])
            p.dma(nthrR[h][:], io["thrRS"].ap()[h:h + 1, ts_],
                  writes=[nthrR[h]])
            p.ts(nthrR[h][:], nthrR[h][:], -1.0, ALU.mult, reads=[nthrR[h]],
                 writes=[nthrR[h]])
        for et in range(32):
            esl = slice(et * 128, (et + 1) * 128)
            for h in range(4):
                p.mm(ps_d[:], R64[0][:, esl], sxeT[h][:], start=True,
                     stop=False, reads=[R64[0], sxeT[h]], writes=[ps_d])
                p.mm(ps_d[:], T64[0][:, esl], syT[h][:], start=False,
                     stop=True, reads=[T64[0], syT[h]], writes=[ps_d])
                p.act(P_sb[:], ps_d[:], AF.Exp, reads=[ps_d], writes=[P_sb])
                p.mm(ps_m[:], R64[0][:, esl], sxeT[h][:], start=True,
                     stop=False, reads=[R64[0], sxeT[h]], writes=[ps_m])
                p.mm(ps_m[:], T64[0][:, esl], syT[h][:], start=False,
                     stop=False, reads=[T64[0], syT[h]], writes=[ps_m])
                p.mm(ps_m[:], c["ones_f"][0:1, :], nthrR[h][0:1, :],
                     start=False, stop=True, reads=[nthrR[h], c["ones_f"]],
                     writes=[ps_m])
                p.ts(M01[:], ps_m[:], 0.0, ALU.is_ge, reads=[ps_m],
                     writes=[M01])
                if h == 0:
                    p.tt(Macc[:], P_sb[:], M01[:], ALU.mult,
                         reads=[P_sb, M01], writes=[Macc])
                else:
                    p.tt(Mtmp[:], P_sb[:], M01[:], ALU.mult,
                         reads=[P_sb, M01], writes=[Mtmp])
                    p.tt(Macc[:], Macc[:], Mtmp[:], ALU.add,
                         reads=[Macc, Mtmp], writes=[Macc])
            for ci in range(8):
                p.mm(ps_w[:], deT[ci][:, esl], xb[ci][:], start=(ci == 0),
                     stop=(ci == 7), reads=[deT[ci], xb[ci]], writes=[ps_w])
            p.act(wsil[:], ps_w[:], AF.Silu, reads=[ps_w], writes=[wsil])
            p.tt(mcb[:], Macc[:], wsil[:], ALU.mult,
                 reads=[Macc, wsil], writes=[mcb])
            p.dma(io["McT"].ap()[esl, ts_], mcb[:], reads=[mcb], q="po")


def emit_b2b2(p, c, io):
    stg = p.sbuf("q_stg", (128, 1024), F32)
    ue = wload(p, "q_ue", io["w_l2_ue"], 4096, D, stage=stg)
    mct = [p.sbuf(f"q_mc{i}", (128, 128), BF16) for i in range(32)]
    ps = p.psum("q_ps", (128, 512), F32)
    ex = p.sbuf("q_ex", (128, 1024), F32)
    mlpv = p.sbuf("q_mlp", (128, 1024), F32)
    gv = p.sbuf("q_gv", (128, 1024), F32)
    bufs = (p.sbuf("q_bxt", (128, 1024), F32),
            p.sbuf("q_bst", (128, 1024), F32),
            p.sbuf("q_bxn", (128, 1024), F32),
            p.sbuf("q_bcol", (128, 8), F32),
            p.sbuf("q_bsq", (128, 1024), F32),
            p.psum("q_bpst", (128, 128), F32),
            p.sbuf("q_btb", (128, 128), F32),
            p.sbuf("q_btbb", (128, 128), BF16))
    for it in range(T // 128):
        r = slice(it * 128, (it + 1) * 128)
        for ci in range(32):
            p.dma(mct[ci][:], io["McT"].ap()[ci * 128:(ci + 1) * 128, r],
                  writes=[mct[ci]])
        for dc in range(2):
            dsl = slice(dc * 512, (dc + 1) * 512)
            for ci in range(32):
                p.mm(ps[:], mct[ci][:], ue[ci][:, dsl], start=(ci == 0),
                     stop=(ci == 31), reads=[mct[ci], ue[ci]], writes=[ps])
            p.copy(ex[:, dsl], ps[:], reads=[ps], writes=[ex])
        p.dma(mlpv[:], io["mlp"].ap()[r, :], writes=[mlpv])
        p.tt(ex[:], ex[:], mlpv[:], ALU.add, reads=[ex, mlpv], writes=[ex])
        p.act(gv[:], ex[:], AF.Silu, reads=[ex], writes=[gv])
        glue_tail(p, c, io, r, gv, bufs, 2)



def emit_a3(p, c, io):
    stg = p.sbuf("a3_stg", (128, 1024), F32)
    Wq1 = wload(p, "Wq1", io["w_l3_Wq1T"], D, D, stage=stg)
    Wk1 = wload(p, "Wk1", io["w_l3_Wk1T"], D, D, stage=stg)
    Wk2 = wload(p, "Wk2", io["w_l3_Wk2T"], D, D, stage=stg)
    bq1 = bias_cols(p, "bq1", io["w_l3_bq1"], D)
    bk1 = bias_cols(p, "bk1", io["w_l3_bk1"], D)
    bk2 = bias_cols(p, "bk2", io["w_l3_bk2"], D)
    onesT = p.sbuf("a3_ones", (128, SEQ), F32)
    p.memset(onesT[:], 1.0, writes=[onesT])
    xb = [p.sbuf(f"a3_xb{i}", (128, SEQ), BF16) for i in range(8)]
    ps = p.psum("a3_ps", (128, SEQ), F32)
    Kt = p.sbuf("a3_K", (128, SEQ), F32)
    Qt = p.sbuf("a3_Q", (128, SEQ), F32)
    K2 = p.sbuf("a3_K2", (128, SEQ), F32)
    Vt = p.sbuf("a3_V", (128, SEQ), F32)
    cum = p.sbuf("a3_cum", (128, SEQ), F32)
    rK = p.sbuf("a3_rK", (128, SEQ), F32)
    w1 = p.sbuf("a3_w1", (128, SEQ), F32)
    w2 = p.sbuf("a3_w2", (128, SEQ), F32)
    w3 = p.sbuf("a3_w3", (128, SEQ), F32)
    ob = p.sbuf("a3_ob", (128, SEQ), BF16)
    b2c = p.sbuf("a3_b2c", (128, 1), F32)
    for b in range(BPC):
        tsl = slice(b * SEQ, (b + 1) * SEQ)
        for ci in range(8):
            p.dma(xb[ci][:], io["xn3Tb"].ap()[ci * 128:(ci + 1) * 128, tsl],
                  writes=[xb[ci]])
        for et in range(8):
            esl = slice(et * 128, (et + 1) * 128)
            p.dma(Vt[:], io["xn3T32"].ap()[esl, tsl], writes=[Vt])
            for ci in range(8):
                mmS(p, ps, slice(None), 0, Wk1[ci][:, esl], xb[ci],
                    slice(None), 0, SEQ, ci == 0, ci == 7,
                    [Wk1[ci], xb[ci]], [ps])
            p.act(Kt[:], ps[:], AF.Exp, bias=bk1[:, et:et + 1],
                  reads=[ps, bk1], writes=[Kt])
            for ci in range(8):
                mmS(p, ps, slice(None), 0, Wq1[ci][:, esl], xb[ci],
                    slice(None), 0, SEQ, ci == 0, ci == 7,
                    [Wq1[ci], xb[ci]], [ps])
            p.act(Qt[:], ps[:], AF.Sigmoid, bias=bq1[:, et:et + 1],
                  reads=[ps, bq1], writes=[Qt])
            for ci in range(8):
                mmS(p, ps, slice(None), 0, Wk2[ci][:, esl], xb[ci],
                    slice(None), 0, SEQ, ci == 0, ci == 7,
                    [Wk2[ci], xb[ci]], [ps])
            p.ts(b2c[:], bk2[:, et:et + 1], 0.000625, ALU.mult,
                 reads=[bk2], writes=[b2c])
            p.act(K2[:, 0:1], b2c[:], AF.Sigmoid, reads=[b2c], writes=[K2])
            p.act(K2[:, 1:SEQ], ps[:, 0:SEQ - 1], AF.Sigmoid,
                  scale=0.000625, bias=b2c[:, 0:1], reads=[ps, b2c],
                  writes=[K2])
            p.scan(cum[:], onesT[:], Kt[:], 0.0, reads=[onesT, Kt],
                   writes=[cum])
            p.ts(cum[:], cum[:], 1e-6, ALU.add, reads=[cum], writes=[cum])
            p.emit("dve", lambda e: e.reciprocal(rK[:], cum[:]),
                   reads=[cum], writes=[rK])
            p.tt(w1[:], Kt[:], Vt[:], ALU.mult, reads=[Kt, Vt], writes=[w1])
            p.scan(cum[:], onesT[:], w1[:], 0.0, reads=[onesT, w1],
                   writes=[cum])
            p.tt(w2[:], cum[:], rK[:], ALU.mult, reads=[cum, rK], writes=[w2])
            p.tt(w1[:], w2[:], Qt[:], ALU.mult, reads=[w2, Qt], writes=[w1])
            p.memset(w2[:, 0:1], 0.0, writes=[w2])
            p.tt(w2[:, 1:SEQ], Vt[:, 1:SEQ], w1[:, 0:SEQ - 1],
                 ALU.subtract, reads=[Vt, w1], writes=[w2])
            p.tt(w2[:], K2[:], w2[:], ALU.mult, reads=[K2, w2], writes=[w2])
            p.scan(w3[:], onesT[:], w2[:], 0.0, reads=[onesT, w2],
                   writes=[w3])
            p.act(cum[:], Qt[:], AF.Relu, scale=-1.0 / 32.0, reads=[Qt],
                  writes=[cum])
            p.ts(w2[:], Qt[:], 0.000625, ALU.mult, reads=[Qt], writes=[w2])
            p.stt(w2[:], cum[:], -0.98, w2[:], ALU.mult, ALU.add,
                  reads=[cum, w2], writes=[w2])
            p.memset(Kt[:, 0:1], 0.0, writes=[Kt])
            p.tt(Kt[:, 1:SEQ], w3[:, 1:SEQ], w2[:, 0:SEQ - 1], ALU.mult,
                 reads=[w3, w2], writes=[Kt])
            p.tt(w1[:], w1[:], Kt[:], ALU.add, reads=[w1, Kt], writes=[w1])
            p.copy(ob[:], w1[:], reads=[w1], writes=[ob])
            p.dma(io["oT"].ap()[esl, tsl], ob[:], reads=[ob], q="po")


def emit_a3b(p, c, io):
    stg = p.sbuf("a3b_stg", (128, 1024), F32)
    Wpr = wload(p, "Wpr", io["w_l3_WprT"], D, D, stage=stg)
    bprr = brow(p, "bprr", io["w_l3_bpr"], D, out_dt=BF16)
    oT = [p.sbuf(f"a3b_oT{i}", (128, 128), BF16) for i in range(8)]
    ps = p.psum("a3b_ps", (128, 512), F32)
    ov = p.sbuf("a3b_ov", (128, 512), F32)
    for it in range(T // 128):
        r = slice(it * 128, (it + 1) * 128)
        for ci in range(8):
            p.dma(oT[ci][:], io["oT"].ap()[ci * 128:(ci + 1) * 128, r],
                  writes=[oT[ci]])
        for dc in range(2):
            dsl = slice(dc * 512, (dc + 1) * 512)
            for ci in range(8):
                p.mm(ps[:], oT[ci][:], Wpr[ci][:, dsl], start=(ci == 0),
                     stop=False, reads=[oT[ci], Wpr[ci]], writes=[ps])
            p.mm(ps[:], c["ones_bf"][0:1, :], bprr[0:1, dsl], start=False,
                 stop=True, reads=[bprr, c["ones_bf"]], writes=[ps])
            p.copy(ov[:], ps[:], reads=[ps], writes=[ov])
            p.dma(io["s_out"].ap()[r, dsl], ov[:], reads=[ov], q="po")



def _cc(a):
    return np.ascontiguousarray(a)


CONST_NAMES = ("ident32", "identbf", "ones_bf", "ones_f", "ones_col")


def tensor_specs():
    s = {
        "x0": ((T, D), F32),
        "out": ((T, D), F32),
        "ident32": ((128, 128), F32), "identbf": ((128, 128), BF16),
        "ones_bf": ((1, 128), BF16), "ones_f": ((1, 128), F32),
        "ones_col": ((128, 1), F32),
        "xzT": ((4096, T), BF16), "xcT": ((2048, T), BF16),
        "dT": ((2048, T), F32), "bcBC": ((32, T), F32),
        "yT": ((2048, T), BF16),
        "resid1": ((T, D), F32), "hfinal": ((T, D), F32),
        "s0T": ((D, T), F32),
        "vT": ((4096, T), BF16),
        "bhT": ((4096, T), BF16),
        "qT": ((D, T), BF16), "kT": ((D, T), BF16), "v1": ((T, D), BF16),
        "rkT0": ((D, SEQ), BF16), "rkT1": ((D, SEQ), BF16),
        "rv0": ((SEQ, D), BF16), "rv1": ((SEQ, D), BF16),
        "rqT": ((D, SEQ), BF16),
        "attout": ((T, D), F32),
        "rowscr": ((16, 128), F32),
        "pastTb0": ((D, SEQ), BF16), "pastTb1": ((D, SEQ), BF16),
        "band": ((128, 192), F32),
        "gT": ((4096, T), BF16), "mlp": ((T, D), F32),
        "McT": ((4096, T), BF16),
        "oT": ((D, T), BF16),
        "weS": ((T, 8), F32), "accS": ((T, D), F32),
        "sxeTS": ((256, T), F32), "syTS": ((256, T), F32),
        "thrRS": ((4, T), F32),
        "w_b0_WfcT": ((D, 4096), F32), "w_b0_lnw": ((4096,), F32),
        "w_b0_lnb": ((4096,), F32), "w_b0_WprT": ((4096, D), F32),
        "w_l2_qwT": ((D, D), F32), "w_l2_kwT": ((D, D), F32),
        "w_l2_vwT": ((D, D), F32), "w_l2_owT": ((D, D), F32),
        "w_l2_qb": ((D,), F32), "w_l2_kb": ((D,), F32),
        "w_l2_vb": ((D,), F32), "w_l2_ob": ((D,), F32),
        "w_l2_Wg": ((D, 4096), F32), "w_l2_Wu": ((D, 4096), F32),
        "w_l2_bg": ((4096,), F32), "w_l2_bu": ((4096,), F32),
        "w_l2_Wd": ((4096, D), F32), "w_l2_bd": ((D,), F32),
        "w_l2_Wq": ((D, 512), F32), "w_l2_keysT": ((128, 512), F32),
        "w_l2_deT": ((D, 4096), F32), "w_l2_ue": ((4096, D), F32),
        "w_l2_R64": ((64, 4096), F32), "w_l2_T64": ((64, 4096), F32),
        "w_l3_Wq1T": ((D, D), F32), "w_l3_Wk1T": ((D, D), F32),
        "w_l3_Wk2T": ((D, D), F32), "w_l3_WprT": ((D, D), F32),
        "w_l3_bq1": ((D,), F32), "w_l3_bk1": ((D,), F32),
        "w_l3_bk2": ((D,), F32), "w_l3_bpr": ((D,), F32),
    }
    for i in range(4):
        s[f"xn{i}"] = ((T, D), F32)
        s[f"xn{i}T32"] = ((D, T), F32)
        s[f"xn{i}Tb"] = ((D, T), BF16)
        s[f"s{i}"] = ((T, D), F32)
        s[f"x{i}"] = ((T, D), F32)
    for i in range(2):
        s[f"w_l0_inpT{i}"] = ((D, 4096), F32)
        s[f"w_l0_convw{i}"] = ((2048, 4), F32)
        s[f"w_l0_convb{i}"] = ((2048,), F32)
        s[f"w_l0_xprojT{i}"] = ((2048, 96), F32)
        s[f"w_l0_dtprojT{i}"] = ((64, 2048), F32)
        s[f"w_l0_dtb{i}"] = ((2048,), F32)
        s[f"w_l0_alog{i}"] = ((2048, 16), F32)
        s[f"w_l0_dd{i}"] = ((2048,), F32)
        s[f"w_l0_outprojT{i}"] = ((2048, 1024), F32)
        s[f"w_l0_rmsw{i}"] = ((1024,), F32)
        s[f"w_l0_lnw{i}"] = ((1024,), F32)
        s[f"w_l0_lnb{i}"] = ((1024,), F32)
    s["w_l0_ffw1T"] = ((D, 8192), F32)
    s["w_l0_ffb1"] = ((8192,), F32)
    s["w_l0_ffw2T"] = ((4096, 1024), F32)
    s["w_l0_ffb2"] = ((1024,), F32)
    for nm in ("Wq", "Wk", "Wv", "Wo", "Wrq", "Wrk", "Wrv"):
        s[f"w_l1_{nm}"] = ((D, D), F32)
    for nm in ("bq", "bk", "bv", "bo", "brq", "brk", "brv", "bg"):
        s[f"w_l1_{nm}"] = ((D,), F32)
    s["w_l1_Wg1"] = ((D, D), F32)
    s["w_l1_Wg2"] = ((D, D), F32)
    s["w_l1_router"] = ((D, 8), F32)
    s["w_l3_router"] = ((D, 8), F32)
    for e in range(8):
        s[f"w_l1_e1_{e}"] = ((D, 4096), F32)
        s[f"w_l1_e2_{e}"] = ((4096, D), F32)
        s[f"w_l1_eb1_{e}"] = ((4096,), F32)
        s[f"w_l1_eb2_{e}"] = ((D,), F32)
        s[f"w_l3_e1_{e}"] = ((D, 4096), F32)
        s[f"w_l3_e2_{e}"] = ((2048, D), F32)
        s[f"w_l3_eb1_{e}"] = ((4096,), F32)
        s[f"w_l3_eb2_{e}"] = ((D,), F32)
    return s


def host_prep(inputs):
    import ml_dtypes
    w = {}
    I = inputs
    w["ident32"] = np.eye(128, dtype=np.float32)
    w["identbf"] = np.eye(128, dtype=np.float32).astype(ml_dtypes.bfloat16)
    w["ones_bf"] = np.ones((1, 128), dtype=ml_dtypes.bfloat16)
    w["ones_f"] = np.ones((1, 128), dtype=np.float32)
    w["ones_col"] = np.ones((128, 1), dtype=np.float32)
    if "L0_attn_in_proj_w" in I:
        for i in range(2):
            w[f"w_l0_inpT{i}"] = _cc(np.asarray(I["L0_attn_in_proj_w"][i]).T)
            w[f"w_l0_convw{i}"] = _cc(np.asarray(I["L0_attn_conv_w"][i]))
            w[f"w_l0_convb{i}"] = _cc(np.asarray(I["L0_attn_conv_b"][i]))
            w[f"w_l0_xprojT{i}"] = _cc(np.asarray(I["L0_attn_x_proj_w"][i]).T)
            w[f"w_l0_dtprojT{i}"] = _cc(np.asarray(I["L0_attn_dt_proj_w"][i]).T)
            w[f"w_l0_dtb{i}"] = _cc(np.asarray(I["L0_attn_dt_proj_b"][i]))
            w[f"w_l0_alog{i}"] = _cc(np.asarray(I["L0_attn_A_log"][i]))
            w[f"w_l0_dd{i}"] = _cc(np.asarray(I["L0_attn_D"][i]))
            w[f"w_l0_outprojT{i}"] = _cc(np.asarray(I["L0_attn_out_proj_w"][i]).T)
            w[f"w_l0_rmsw{i}"] = _cc(np.asarray(I["L0_attn_rms_w"][i]))
            w[f"w_l0_lnw{i}"] = _cc(np.asarray(I["L0_attn_ln_w"][i]))
            w[f"w_l0_lnb{i}"] = _cc(np.asarray(I["L0_attn_ln_b"][i]))
        w["w_l0_ffw1T"] = _cc(np.asarray(I["L0_attn_ff_w1"]).T)
        w["w_l0_ffb1"] = _cc(np.asarray(I["L0_attn_ff_b1"]))
        w["w_l0_ffw2T"] = _cc(np.asarray(I["L0_attn_ff_w2"]).T)
        w["w_l0_ffb2"] = _cc(np.asarray(I["L0_attn_ff_b2"]))
    if "L0_ffn_W_fc" in I:
        w["w_b0_WfcT"] = _cc(np.asarray(I["L0_ffn_W_fc"]).T)
        w["w_b0_lnw"] = _cc(np.asarray(I["L0_ffn_ln_w"]))
        w["w_b0_lnb"] = _cc(np.asarray(I["L0_ffn_ln_b"]))
        w["w_b0_WprT"] = _cc(np.asarray(I["L0_ffn_W_proj"]).T)
    if "L1_attn_Wq" in I:
        for nm in ("Wq", "Wk", "Wv", "Wo", "Wrq", "Wrk", "Wrv"):
            w[f"w_l1_{nm}"] = _cc(np.asarray(I[f"L1_attn_{nm}"]))
        for nm in ("bq", "bk", "bv", "bo", "brq", "brk", "brv", "bg"):
            w[f"w_l1_{nm}"] = _cc(np.asarray(I[f"L1_attn_{nm}"]))
        w["w_l1_Wg1"] = _cc(np.asarray(I["L1_attn_Wg"][0:1024]))
        w["w_l1_Wg2"] = _cc(np.asarray(I["L1_attn_Wg"][1024:2048]))
        pT = np.asarray(I["L1_attn_past_states"])
        import ml_dtypes as _mld
        w["pastTb0"] = _cc(pT[0, 0].T).astype(_mld.bfloat16)
        w["pastTb1"] = _cc(pT[1, 0].T).astype(_mld.bfloat16)
        w["w_l1_router"] = _cc(np.asarray(I["L1_ffn_router_w"]).T)
        for e in range(8):
            w[f"w_l1_e1_{e}"] = _cc(np.asarray(I["L1_ffn_w1"][e]).T)
            w[f"w_l1_e2_{e}"] = _cc(np.asarray(I["L1_ffn_w2"][e]).T)
            w[f"w_l1_eb1_{e}"] = _cc(np.asarray(I["L1_ffn_b1"][e]))
            w[f"w_l1_eb2_{e}"] = _cc(np.asarray(I["L1_ffn_b2"][e]))
    if "L2_attn_q_w" in I:
        for nm, src_ in (("qw", "q_w"), ("kw", "k_w"), ("vw", "v_w"),
                         ("ow", "o_w")):
            w[f"w_l2_{nm}T"] = _cc(np.asarray(I[f"L2_attn_{src_}"]).T)
        for nm, src_ in (("qb", "q_b"), ("kb", "k_b"), ("vb", "v_b"),
                         ("ob", "o_b")):
            w[f"w_l2_{nm}"] = _cc(np.asarray(I[f"L2_attn_{src_}"]))
        ii = np.arange(128)[:, None]
        jj = np.arange(192)[None, :]
        dd_ = jj - ii
        w["band"] = np.where((dd_ >= 0) & (dd_ <= 64), 0.0,
                             -1e9).astype(np.float32)
        w["w_l2_Wg"] = _cc(np.asarray(I["L2_ffn_Wg"]))
        w["w_l2_Wu"] = _cc(np.asarray(I["L2_ffn_Wu"]))
        w["w_l2_bg"] = _cc(np.asarray(I["L2_ffn_bg"]))
        w["w_l2_bu"] = _cc(np.asarray(I["L2_ffn_bu"]))
        w["w_l2_Wd"] = _cc(np.asarray(I["L2_ffn_Wd"]))
        w["w_l2_bd"] = _cc(np.asarray(I["L2_ffn_bd"]))
        w["w_l2_Wq"] = _cc(np.asarray(I["L2_ffn_Wq"]))
        ks_ = np.asarray(I["L2_ffn_keys"])
        kt_ = np.zeros((64, 512), np.float32)
        for h in range(4):
            for pp in range(2):
                kt_[:, (h * 2 + pp) * 64:(h * 2 + pp) * 64 + 64] = \
                    ks_[h, :, pp, :].T
        w["w_l2_keysT"] = np.concatenate([kt_, kt_], axis=0)
        w["w_l2_deT"] = _cc(np.asarray(I["L2_ffn_down_embed"]).T)
        w["w_l2_ue"] = _cc(np.asarray(I["L2_ffn_up_embed"]))
        R64 = np.zeros((64, 4096), np.float32)
        T64 = np.zeros((64, 4096), np.float32)
        for i in range(64):
            R64[i, i * 64:(i + 1) * 64] = 1.0
            T64[i, i::64] = 1.0
        w["w_l2_R64"] = R64
        w["w_l2_T64"] = T64
    if "L3_attn_Wq1" in I:
        w["w_l3_Wq1T"] = _cc(np.asarray(I["L3_attn_Wq1"]).T)
        w["w_l3_Wk1T"] = _cc(np.asarray(I["L3_attn_Wk1"]).T)
        w["w_l3_Wk2T"] = _cc(np.asarray(I["L3_attn_Wk2"]).T)
        w["w_l3_WprT"] = _cc(np.asarray(I["L3_attn_Wproj"]).T)
        w["w_l3_bq1"] = _cc(np.asarray(I["L3_attn_bq1"]))
        w["w_l3_bk1"] = _cc(np.asarray(I["L3_attn_bk1"]))
        w["w_l3_bk2"] = _cc(np.asarray(I["L3_attn_bk2"]))
        w["w_l3_bpr"] = _cc(np.asarray(I["L3_attn_bproj"]))
        w["w_l3_router"] = _cc(np.asarray(I["L3_ffn_gate_w"]))
        for e in range(8):
            d1 = np.asarray(I["L3_ffn_dense_1_w"][e])
            d1p = np.concatenate([d1[0::2], d1[1::2]], axis=0)
            w[f"w_l3_e1_{e}"] = _cc(d1p.T)
            b1 = np.asarray(I["L3_ffn_dense_1_b"][e])
            w[f"w_l3_eb1_{e}"] = _cc(np.concatenate([b1[0::2], b1[1::2]]))
            w[f"w_l3_e2_{e}"] = _cc(np.asarray(I["L3_ffn_dense_2_w"][e]).T)
            w[f"w_l3_eb2_{e}"] = _cc(np.asarray(I["L3_ffn_dense_2_b"][e]))
    return w


def stage_glue(p, c, io, i, first=False, last=False):
    if first:
        emit_glue(p, c, io["x_in"].ap(), None, None, None,
                  io[f"xn{i}"].ap(), io[f"xn{i}T32"].ap(), io[f"xn{i}Tb"].ap())


def build_stage(emit, io_names, out_names):
    specs = tensor_specs()
    p = P()
    dr = {}
    io = {}
    ing, outg = [], []
    for local, glob in io_names.items():
        shape, dt_ = specs[glob]
        kind = "ExternalOutput" if local in out_names else "ExternalInput"
        h = p.dram(glob, shape, dt_, kind=kind)
        io[local] = h
        (outg if local in out_names else ing).append(glob)
    for cn in CONST_NAMES:
        shape, dt_ = specs[cn]
        dr[cn] = p.dram(cn, shape, dt_, kind="ExternalInput")
        ing.append(cn)
    c = common_consts(p, dr)
    emit(p, c, io)
    p.barrier_outputs()
    nc = p.build()
    return p, nc, ing, outg


def run_stage(emit, io_names, out_names, vals, ncore=NCORES):
    p, nc, ing, outg = build_stage(emit, io_names, out_names)
    in_maps = []
    for ci in range(ncore):
        m = {"cpool": p.cpool_host()}
        for g in ing:
            v = vals[g]
            m[g] = v[ci] if isinstance(v, list) else v
        in_maps.append(m)
    res = run_bass_kernel_spmd(nc, in_maps, core_ids=list(range(ncore)))
    for g in outg:
        vals[g] = [np.asarray(res.results[ci][g]) for ci in range(ncore)]
    return res


def l0_stage_list():
    st = []
    st.append((lambda p, c, io: stage_glue(p, c, io, 0, first=True),
               {"x_in": "x0", "xn0": "xn0", "xn0T32": "xn0T32",
                "xn0Tb": "xn0Tb"},
               {"xn0", "xn0T32", "xn0Tb"}))
    for li in range(2):
        rin = "xn0" if li == 0 else "resid1"
        st.append((lambda p, c, io, li=li: emit_m1(p, c, io, li),
                   dict({"resid_in": rin, "xzT": "xzT",
                         f"w_l0_rmsw{li}": f"w_l0_rmsw{li}",
                         f"w_l0_inpT{li}": f"w_l0_inpT{li}"}),
                   {"xzT"}))
        st.append((lambda p, c, io, li=li: emit_m2(p, c, io, li),
                   dict({"xzT": "xzT", "xcT": "xcT", "dT": "dT",
                         "bcBC": "bcBC",
                         f"w_l0_convw{li}": f"w_l0_convw{li}",
                         f"w_l0_convb{li}": f"w_l0_convb{li}",
                         f"w_l0_xprojT{li}": f"w_l0_xprojT{li}",
                         f"w_l0_dtprojT{li}": f"w_l0_dtprojT{li}",
                         f"w_l0_dtb{li}": f"w_l0_dtb{li}"}),
                   {"xcT", "dT", "bcBC"}))
        st.append((lambda p, c, io, li=li: emit_m3(p, c, io, li),
                   dict({"xzT": "xzT", "xcT": "xcT", "dT": "dT",
                         "bcBC": "bcBC", "yT": "yT",
                         f"w_l0_alog{li}": f"w_l0_alog{li}",
                         f"w_l0_dd{li}": f"w_l0_dd{li}"}),
                   {"yT"}))
        rout = "resid1" if li == 0 else "hfinal"
        st.append((lambda p, c, io, li=li: emit_m4(p, c, io, li,
                                                   resid_add=(li == 0)),
                   dict({"yT": "yT", "resid_in": rin, "resid_out": rout,
                         f"w_l0_outprojT{li}": f"w_l0_outprojT{li}",
                         f"w_l0_lnw{li}": f"w_l0_lnw{li}",
                         f"w_l0_lnb{li}": f"w_l0_lnb{li}"}),
                   {"resid_out"}))
    st.append((lambda p, c, io: emit_mffn_a(p, c, io),
               {"hfinal": "hfinal", "vT": "vT",
                "w_l0_ffw1T": "w_l0_ffw1T", "w_l0_ffb1": "w_l0_ffb1"},
               {"vT"}))
    st.append((lambda p, c, io: emit_mffn_b(p, c, io),
               {"vT": "vT", "s0T": "s0T", "s0": "s0",
                "w_l0_ffw2T": "w_l0_ffw2T", "w_l0_ffb2": "w_l0_ffb2"},
               {"s0T", "s0"}))
    return st


def all_stage_list():
    st = l0_stage_list()
    st.append((lambda p, c, io: emit_b0a(p, c, io),
               {"xn0Tb": "xn0Tb", "bhT": "bhT", "w_b0_WfcT": "w_b0_WfcT",
                "w_b0_lnw": "w_b0_lnw", "w_b0_lnb": "w_b0_lnb"}, {"bhT"}))
    st.append((lambda p, c, io: emit_b0b(p, c, io),
               {"bhT": "bhT", "x0": "x0", "s0": "s0", "x1": "x1",
                "xn1": "xn1", "xn1T32": "xn1T32", "xn1Tb": "xn1Tb",
                "w_b0_WprT": "w_b0_WprT"},
               {"x1", "xn1", "xn1T32", "xn1Tb"}))
    a1w = {f"w_l1_{n}": f"w_l1_{n}" for n in
           ("Wq", "Wk", "Wv", "Wrk", "Wrv", "bq", "bk", "bv", "brk", "brv")}
    st.append((lambda p, c, io: emit_a1a(p, c, io),
               dict(a1w, **{"xn1Tb": "xn1Tb", "pastTb0": "pastTb0",
                            "pastTb1": "pastTb1", "qT": "qT", "kT": "kT",
                            "v1": "v1", "rkT0": "rkT0", "rkT1": "rkT1",
                            "rv0": "rv0", "rv1": "rv1"}),
               {"qT", "kT", "v1", "rkT0", "rkT1", "rv0", "rv1"}))
    st.append((lambda p, c, io: emit_a1b(p, c, io),
               {"qT": "qT", "kT": "kT", "v1": "v1", "attout": "attout",
                "rowscr": "rowscr", "w_l1_Wo": "w_l1_Wo",
                "w_l1_bo": "w_l1_bo"},
               {"attout", "rowscr"}))
    st.append((lambda p, c, io: emit_a1c(p, c, io),
               {"attout": "attout", "rqT": "rqT", "rkT0": "rkT0",
                "rkT1": "rkT1", "rv0": "rv0", "rv1": "rv1",
                "rowscr": "rowscr", "s_out": "s1",
                "w_l1_Wrq": "w_l1_Wrq", "w_l1_brq": "w_l1_brq",
                "w_l1_Wg1": "w_l1_Wg1", "w_l1_Wg2": "w_l1_Wg2",
                "w_l1_bg": "w_l1_bg"},
               {"s_out", "rqT", "rowscr"}))
    moe1 = {"weS": "weS", "accS": "accS",
            "w_router": "w_l1_router", "xnT32_in": "xn1T32",
            "xnTb_in": "xn1Tb", "x_in": "x1", "s_in": "s1",
            "x_out": "x2", "xn_out": "xn2", "xnT32_out": "xn2T32",
            "xnTb_out": "xn2Tb"}
    for e in range(8):
        moe1[f"w_e1_{e}"] = f"w_l1_e1_{e}"
        moe1[f"w_e2_{e}"] = f"w_l1_e2_{e}"
        moe1[f"w_eb1_{e}"] = f"w_l1_eb1_{e}"
        moe1[f"w_eb2_{e}"] = f"w_l1_eb2_{e}"
    st.append((lambda p, c, io: emit_moe(p, c, io, 8, 4096, l3=False),
               moe1, {"x_out", "xn_out", "xnT32_out", "xnTb_out", "weS",
                      "accS"}))
    a2w = {f"w_l2_{n}": f"w_l2_{n}" for n in
           ("qwT", "kwT", "vwT", "owT", "qb", "kb", "vb", "ob")}
    st.append((lambda p, c, io: emit_a2(p, c, io),
               dict(a2w, **{"xn2Tb": "xn2Tb", "band": "band", "qT": "qT",
                            "kT": "kT", "v1": "v1", "s_out": "s2"}),
               {"s_out", "qT", "kT", "v1"}))
    st.append((lambda p, c, io: emit_b2a(p, c, io),
               {"xn2Tb": "xn2Tb", "gT": "gT",
                "w_l2_Wg": "w_l2_Wg", "w_l2_Wu": "w_l2_Wu",
                "w_l2_bg": "w_l2_bg", "w_l2_bu": "w_l2_bu"},
               {"gT"}))
    st.append((lambda p, c, io: emit_b2a2(p, c, io),
               {"gT": "gT", "mlp": "mlp", "w_l2_Wd": "w_l2_Wd",
                "w_l2_bd": "w_l2_bd"},
               {"mlp"}))
    st.append((lambda p, c, io: emit_b2b1(p, c, io),
               {"xn2T32": "xn2T32", "w_l2_Wq": "w_l2_Wq",
                "w_l2_keysT": "w_l2_keysT", "sxeTS": "sxeTS",
                "syTS": "syTS", "thrRS": "thrRS"},
               {"sxeTS", "syTS", "thrRS"}))
    st.append((lambda p, c, io: emit_b2b1d(p, c, io),
               {"xn2Tb": "xn2Tb", "McT": "McT", "sxeTS": "sxeTS",
                "syTS": "syTS", "thrRS": "thrRS", "w_l2_deT": "w_l2_deT",
                "w_l2_R64": "w_l2_R64", "w_l2_T64": "w_l2_T64"},
               {"McT"}))
    st.append((lambda p, c, io: emit_b2b2(p, c, io),
               {"McT": "McT", "mlp": "mlp", "x_in": "x2", "s_in": "s2",
                "x_out": "x3", "xn_out": "xn3", "xnT32_out": "xn3T32",
                "xnTb_out": "xn3Tb", "w_l2_ue": "w_l2_ue"},
               {"x_out", "xn_out", "xnT32_out", "xnTb_out"}))
    st.append((lambda p, c, io: emit_a3(p, c, io),
               {"xn3Tb": "xn3Tb", "xn3T32": "xn3T32", "oT": "oT",
                "w_l3_Wq1T": "w_l3_Wq1T", "w_l3_Wk1T": "w_l3_Wk1T",
                "w_l3_Wk2T": "w_l3_Wk2T",
                "w_l3_bq1": "w_l3_bq1", "w_l3_bk1": "w_l3_bk1",
                "w_l3_bk2": "w_l3_bk2"},
               {"oT"}))
    st.append((lambda p, c, io: emit_a3b(p, c, io),
               {"oT": "oT", "s_out": "s3", "w_l3_WprT": "w_l3_WprT",
                "w_l3_bpr": "w_l3_bpr"},
               {"s_out"}))
    moe3 = {"weS": "weS", "accS": "accS",
            "w_router": "w_l3_router", "xnT32_in": "xn3T32",
            "xnTb_in": "xn3Tb", "x_in": "x3", "s_in": "s3",
            "x_out": "out"}
    for e in range(8):
        moe3[f"w_e1_{e}"] = f"w_l3_e1_{e}"
        moe3[f"w_e2_{e}"] = f"w_l3_e2_{e}"
        moe3[f"w_eb1_{e}"] = f"w_l3_eb1_{e}"
        moe3[f"w_eb2_{e}"] = f"w_l3_eb2_{e}"
    st.append((lambda p, c, io: emit_moe(p, c, io, 8, 2048, l3=True),
               moe3, {"x_out", "weS", "accS"}))
    return st


def kernel_multi(x, w):
    vals = dict(w)
    vals["x0"] = [_cc(x[2 * ci:2 * ci + 2].reshape(T, D))
                  for ci in range(NCORES)]
    for (emit, io_names, out_names) in all_stage_list():
        run_stage(emit, io_names, out_names, vals)
    out = np.concatenate([vals["out"][ci].reshape(2, SEQ, D)
                          for ci in range(NCORES)], axis=0)
    return out.astype(np.float32)


def kernel_fused(x, w):
    specs = tensor_specs()
    p = P()
    p.enable_pooling()
    host_names = set(w.keys()) | {"x0"}
    handles = {}

    def get_handle(glob):
        if glob in handles:
            return handles[glob]
        shape, dt_ = specs[glob]
        if glob == "out":
            kind = "ExternalOutput"
        elif glob in host_names:
            kind = "ExternalInput"
        else:
            kind = "Internal"
        handles[glob] = p.dram(glob, shape, dt_, kind=kind)
        return handles[glob]

    dr = {cn: get_handle(cn) for cn in CONST_NAMES}
    c = common_consts(p, dr)
    for (emit, io_names, out_names) in all_stage_list():
        p.begin_stage()
        io = {local: get_handle(glob) for local, glob in io_names.items()}
        emit(p, c, io)
    p.barrier_outputs()
    nc = p.build()
    in_maps = []
    for ci in range(NCORES):
        m = {"cpool": p.cpool_host()}
        for glob, h in handles.items():
            if glob == "out" or glob not in host_names:
                continue
            m[glob] = (_cc(x[2 * ci:2 * ci + 2].reshape(T, D))
                       if glob == "x0" else w[glob])
        in_maps.append(m)
    res = run_bass_kernel_spmd(nc, in_maps, core_ids=list(range(NCORES)))
    out = np.concatenate([np.asarray(res.results[ci]["out"]).reshape(2, SEQ, D)
                          for ci in range(NCORES)], axis=0)
    return out.astype(np.float32)


def kernel(**inputs):
    x = np.asarray(inputs["L0_attn_x"], dtype=np.float32)
    w = host_prep(inputs)
    import os as _os
    if _os.environ.get("MK_FUSED"):
        return kernel_fused(x, w)
    return kernel_multi(x, w)
